# version 30
# baseline (speedup 1.0000x reference)
import numpy as np
import concourse.bass as bass
import concourse.mybir as mybir

F32 = mybir.dt.float32
BF16 = mybir.dt.bfloat16
AF = mybir.ActivationFunctionType
ALU = mybir.AluOpType
AX = mybir.AxisListType

T = 4096
DM = 1024
NCH = DM // 128


class Ser:

    def __init__(self, nc):
        self.nc = nc
        self.ops = []
        self.total = 0

    def add(self, eng, fn, after=None, inc=1):
        self.ops.append((eng, self.total, inc, fn, after))
        self.total += inc

    def dma(self, eng, fn, after=None):
        self.ops.append((eng, self.total, 16, fn, after))
        self.total += 16

    def v(self, fn, after=None, inc=1): self.add("vector", fn, after, inc)
    def s(self, fn, after=None, inc=1): self.add("scalar", fn, after, inc)
    def t(self, fn, after=None, inc=1): self.add("tensor", fn, after, inc)
    def g(self, fn, after=None, inc=1): self.add("gpsimd", fn, after, inc)
    def dmas(self, fn, after=None): self.dma("sync", fn, after)
    def dmag(self, fn, after=None): self.dma("gpsimd", fn, after)

    SEG = 48000
    STREAMS = ("vector", "scalar", "tensor", "gpsimd", "sync")

    def emit(self, block, semgrid, clear_sems=None):
        SEG = Ser.SEG
        S = Ser.STREAMS
        pre = []
        snap = {0: dict.fromkeys(S, 0)}
        cur = dict.fromkeys(S, 0)
        tot = 0
        for (en, pos, inc, fn, after) in self.ops:
            pre.append(dict(cur))
            cur[en] += inc
            tot += inc
            snap[tot] = dict(cur)
        finals = dict(cur)
        for y in S:
            need = finals[y] // SEG + 1
            assert need <= len(semgrid[y]), (y, finals[y], len(semgrid[y]))

        def wait_upto(eng, covered, y, target):
            if target <= covered[y]:
                return
            sems = semgrid[y]
            j = (target - 1) // SEG
            cj = (covered[y] - 1) // SEG if covered[y] > 0 else 0
            for jj in range(cj, j):
                full = min(finals[y] - jj * SEG, SEG)
                eng.wait_ge(sems[jj], full)
            eng.wait_ge(sems[j], target - j * SEG)
            covered[y] = target

        aug = []
        prev = None
        for idx, (en, pos, inc, fn, after) in enumerate(self.ops):
            aug.append((en, idx, inc, fn, after, prev))
            prev = en
        for name in S:
            ops = [o for o in aug if o[0] == name]
            if not ops and name != "sync":
                continue

            def body(eng, ops=ops, name=name):
                covered = dict.fromkeys(S, 0)
                own = 0
                is_dma_q = name in ("gpsimd", "sync")
                for (_, idx, inc, fn, after, pe_) in ops:
                    dep = pre[idx] if after is None else snap[after]
                    skip = (after is None and name == "tensor"
                            and pe_ == "tensor")
                    if not skip:
                        for y in S:
                            if y == name and name == "tensor":
                                continue
                            wait_upto(eng, covered, y, dep[y])
                    if is_dma_q:
                        wait_upto(eng, covered, name, own)
                    ins = fn(eng)
                    if inc:
                        j = own // SEG
                        ins.then_inc(semgrid[name][j], inc)
                        own += inc
                if name == "sync":
                    for y in S:
                        wait_upto(eng, covered, y, finals[y])
                    for cs in (clear_sems or sum(semgrid.values(), [])):
                        eng.sem_clear(cs)

            getattr(block, name)(body)


class Common:

    def __init__(self, nc, MT=1024):
        self.MT = MT
        al = nc.alloc_sbuf_tensor
        self.ones_1p = al("C_ones1p", [1, 128], F32)
        self.ones128 = al("C_ones128", [128, 1], F32)
        self.r0 = al("C_r0", [1, MT + 256], F32)
        self.r1 = al("C_r1", [1, MT + 256], F32)
        self.r2 = al("C_r2", [1, MT], F32)
        self.sq = al("C_sq", [128, 512], F32)
        ap = nc.alloc_psum_tensor
        self.ps = [ap(f"C_ps{i}", [128, 512], F32) for i in range(7)]
        self.psbf = ap("C_psbf", [128, 256], BF16)

    def load(self, ser, dram):
        ser.dmas(lambda e: e.dma_start(out=self.ones_1p[:, :], in_=dram["ones_1p"][:, :]))
        ser.dmas(lambda e: e.dma_start(out=self.ones128[:, :], in_=dram["ones128"][:, :]))


def emit_ln(ser, C, res, nch, MT, out_dma=None):
    psr0, psr1 = C.ps[3], C.ps[4]
    psb0, psb1 = C.ps[5], C.ps[6]
    NTT = MT // 512
    DMf = nch * 128
    for tt in range(NTT):
        sl = slice(tt * 512, (tt + 1) * 512)
        for oc in range(nch):
            ser.t(lambda e, oc=oc, sl=sl: e.matmul(
                psr0[0:1, :], C.ones128[:, :], res[:, oc, sl],
                start=(oc == 0), stop=(oc == nch - 1)))
        for oc in range(nch):
            ser.s(lambda e, oc=oc, sl=sl: e.activation(
                C.sq[:, 0:512], res[:, oc, sl], AF.Square))
            ser.t(lambda e, oc=oc, sl=sl: e.matmul(
                psr1[0:1, :], C.ones128[:, :], C.sq[:, 0:512],
                start=(oc == 0), stop=(oc == nch - 1)))
        ser.s(lambda e, sl=sl: e.copy(C.r0[:, sl], psr0[0:1, :]))
        ser.s(lambda e, sl=sl: e.copy(C.r1[:, sl], psr1[0:1, :]))
    ser.v(lambda e: e.tensor_scalar_mul(C.r0[:, 0:MT], C.r0[:, 0:MT], 1.0 / DMf))
    ser.v(lambda e: e.tensor_scalar(C.r1[:, 0:MT], C.r1[:, 0:MT], 1.0 / DMf, 1e-5,
                                    ALU.mult, ALU.add))
    ser.v(lambda e: e.tensor_tensor(C.r2[:, 0:MT], C.r0[:, 0:MT], C.r0[:, 0:MT],
                                    ALU.mult))
    ser.v(lambda e: e.tensor_tensor(C.r1[:, 0:MT], C.r1[:, 0:MT], C.r2[:, 0:MT],
                                    ALU.subtract))
    ser.s(lambda e: e.activation(C.r1[:, 0:MT], C.r1[:, 0:MT], AF.Ln))
    ser.s(lambda e: e.activation(C.r1[:, 0:MT], C.r1[:, 0:MT], AF.Exp, scale=-0.5))
    ser.v(lambda e: e.scalar_tensor_tensor(C.r2[:, 0:MT], C.r0[:, 0:MT], -1.0,
                                           C.r1[:, 0:MT], ALU.mult, ALU.mult))
    for tt in range(NTT):
        sl = slice(tt * 512, (tt + 1) * 512)
        ser.t(lambda e, sl=sl: e.matmul(psb0[:, :], C.ones_1p[:, :], C.r1[:, sl],
                                        start=True, stop=True))
        ser.t(lambda e, sl=sl: e.matmul(psb1[:, :], C.ones_1p[:, :], C.r2[:, sl],
                                        start=True, stop=True))
        for oc in range(nch):
            ser.v(lambda e, oc=oc, sl=sl: e.tensor_tensor(
                res[:, oc, sl], res[:, oc, sl], psb0[:, :], ALU.mult))
            ser.v(lambda e, oc=oc, sl=sl: e.tensor_tensor(
                res[:, oc, sl], res[:, oc, sl], psb1[:, :], ALU.add))
    if out_dma is not None:
        ser.dmas(out_dma)

def ceil_div(a, b):
    return (a + b - 1) // b


PIPE = {"A": False, "B": False, "Dq": False, "Dk": False, "Dwo": False, "Dw1": False, "Dw2": False}


def stream_loop(ser, nocs, ntiles, nkc, dma_fn, mm_fn, evac_fn, enabled=True):
    mmg_last = {}
    evac_by_bank = {}
    dma_ov = enabled in (True, "dma")
    ev_ov = enabled in (True, "evac")
    for oc in range(nocs):
        par = oc % 2
        ser.dma("gpsimd", dma_fn(oc, par),
                after=(mmg_last.get(oc - 2) if dma_ov else None))
        dma_end = ser.total
        for tt in range(ntiles):
            bank = tt % 2
            deps = [dma_end]
            if bank in evac_by_bank:
                deps.append(evac_by_bank[bank])
            for kc in range(nkc):
                ser.add("tensor", mm_fn(oc, tt, kc, par, bank),
                        after=(max(deps) if (kc == 0 and (dma_ov or ev_ov))
                               else None),
                        inc=(1 if kc == nkc - 1 else 0))
            mmg_last[oc] = ser.total
            first = True
            for eng, fn in evac_fn(oc, tt, bank):
                ser.add(eng, fn,
                        after=(mmg_last[oc] if (first and ev_ov) else None))
                first = False
            evac_by_bank[bank] = ser.total



def build_l0_attn(ser, nc, C, dram, dbg=None):
    dbg = dbg or {}
    def dump(name, buf):
        if name in dbg:
            h = dbg[name]
            ser.dmas(lambda e, h=h, buf=buf: e.dma_start(out=h[:, :], in_=buf))
    """dram: dict of DRAM tensor handles:
      x_fm (1024,4096) f32 in ; out_fm (1024,4096) f32 out
      w_in_t (1024,4096) ; w_out_t (2048,1024) ; w_x_aug (2048,65)
      convw_s (128,16,4) ; convb_s (128,16) ; d_s (128,16) ; alog_s (1,16)
      ones_1p (1,128) f32 ; ones16 (16,1) f32 ; ones128 (128,1) f32
    """
    MT = 1024
    NMT = T // MT
    NTT = MT // 512

    al = nc.alloc_sbuf_tensor
    xbf = al("A_xbf", [128, NCH, MT], BF16)
    wbuf = al("A_wbuf", [128, 2, 8, 128], BF16)
    wxbuf = al("A_wxbuf", [128, 16, 65], BF16)
    wobuf = al("A_wobuf", [128, 2, 16, 128], BF16)
    convw = al("A_convw", [128, 16, 4], F32)
    convb = al("A_convb", [128, 16], F32)
    dsb = al("A_dsb", [128, 16], F32)
    negA = al("A_negA", [1, 16], F32)
    ones_1p = C.ones_1p
    ones16 = al("A_ones16", [16, 1], F32)
    ones128 = C.ones128
    xc = al("A_xc", [128, 16, 3 + MT], BF16)
    carry = al("A_carry", [128, 16, 3], BF16)
    z = al("A_z", [128, 16, MT], BF16)
    convacc = al("A_convacc", [128, MT], F32)
    cbuf = al("A_cbuf", [16, MT], F32)
    delta = al("A_delta", [1, MT], F32)
    sprow = al("A_sprow", [1, MT], F32)
    dA = al("A_dA", [16, MT], F32)
    dBx = al("A_dBx", [16, MT], F32)
    hseq = dBx
    hcol = al("A_hcol", [16, 1], F32)
    yrow = al("A_yrow", [1, MT], F32)
    zsil = al("A_zsil", [128, 512], BF16)
    res = al("A_res", [128, NCH, MT], F32)
    srow = delta
    sqrow = sprow
    murow = yrow
    rstd = dA[0:1, :]
    nmur = dBx[0:1, :]

    ps0 = C.ps[0]
    ps1 = C.ps[1]
    psx = C.ps[2][0:65, :]
    ps16 = C.ps[3][0:16, :]
    psr0 = C.ps[4][0:1, :]
    psr1 = C.ps[4][32:33, :]
    psb0 = C.ps[5]
    psb1 = C.ps[6]

    x_fm, out_fm = dram["x_fm"], dram["out_fm"]
    w_in_t, w_out_t, w_x_aug = dram["w_in_t"], dram["w_out_t"], dram["w_x_aug"]

    ser.dmag(lambda e: e.dma_start(out=wxbuf[:, :, :],
                                   in_=w_x_aug.rearrange("(c p) f -> p c f", p=128)))
    ser.dmas(lambda e: e.dma_start(out=convw[:, :, :], in_=dram["convw_s"][:, :, :]))
    ser.dmas(lambda e: e.dma_start(out=convb[:, :], in_=dram["convb_s"][:, :]))
    ser.dmas(lambda e: e.dma_start(out=dsb[:, :], in_=dram["d_s"][:, :]))
    ser.dmas(lambda e: e.dma_start(out=negA[:, :], in_=dram["alog_s"][:, :]))
    ser.dmas(lambda e: e.dma_start(out=ones16[:, :], in_=dram["ones16"][:, :]))
    ser.s(lambda e: e.activation(negA[:, :], negA[:, :], AF.Exp))
    ser.v(lambda e: e.tensor_scalar_mul(negA[:, :], negA[:, :], -1.0))
    ser.v(lambda e: e.memset(xc[:, :, 0:3], 0.0))

    for mt in range(NMT):
        t0 = mt * MT
        ser.dmag(lambda e, t0=t0: e.dma_start(
            out=xbf[:, :, :],
            in_=x_fm.rearrange("(c p) t -> p c t", p=128)[:, :, t0:t0 + MT]))

        def _xz_dst(oc, tt):
            if oc < 16:
                return xc[:, oc, 3 + tt * 512: 3 + (tt + 1) * 512]
            return z[:, oc - 16, tt * 512:(tt + 1) * 512]
        stream_loop(
            ser, 32, NTT, 8,
            lambda oc, par: (lambda e: e.dma_start(
                out=wbuf[:, par, :, :],
                in_=w_in_t[:, oc * 128:(oc + 1) * 128].rearrange(
                    "(c p) f -> p c f", p=128))),
            lambda oc, tt, kc, par, bank: (lambda e: e.matmul(
                C.ps[bank][:, :], wbuf[:, par, kc, :],
                xbf[:, kc, tt * 512:(tt + 1) * 512],
                start=(kc == 0), stop=(kc == 7))),
            lambda oc, tt, bank: [("scalar", lambda e: e.copy(
                _xz_dst(oc, tt), C.ps[bank][:, :]))], enabled=PIPE["A"])

        if mt == 0:
            dump("xzraw", xc[:, 0, 3:3 + MT])
            dump("zraw", z[:, 0, 0:MT])
        if mt < NMT - 1:
            ser.v(lambda e: e.tensor_copy(carry[:, :, :], xc[:, :, MT:MT + 3]))
        for c in range(16):
            ser.v(lambda e, c=c: e.tensor_scalar(
                convacc[:, :], xc[:, c, 0:MT], convw[:, c, 0:1], None, ALU.mult))
            for k in range(1, 4):
                ser.v(lambda e, c=c, k=k: e.scalar_tensor_tensor(
                    convacc[:, :], xc[:, c, k:k + MT],
                    convw[:, c, k:k + 1], convacc[:, :], ALU.mult, ALU.add))
            ser.s(lambda e, c=c: e.activation(
                xc[:, c, 3:3 + MT], convacc[:, :], AF.Silu,
                bias=convb[:, c:c + 1]))
        if mt < NMT - 1:
            ser.v(lambda e: e.tensor_copy(xc[:, :, 0:3], carry[:, :, :]))

        for tt in range(NTT):
            for kc in range(16):
                ser.t(lambda e, tt=tt, kc=kc: e.matmul(
                    psx[:, :], wxbuf[:, kc, :],
                    xc[:, kc, 3 + tt * 512:3 + (tt + 1) * 512],
                    start=(kc == 0), stop=(kc == 15)))
            sl = slice(tt * 512, (tt + 1) * 512)
            ser.s(lambda e, sl=sl: e.copy(dBx[:, sl], psx[0:16, :]))
            ser.s(lambda e, sl=sl: e.copy(cbuf[:, sl], psx[32:48, :]))
            ser.s(lambda e, sl=sl: e.copy(yrow[:, sl], psx[64:65, :]))

        if mt == 0:
            dump("xcpost", xc[:, 0, 3:3 + MT])
            dump("bm", dBx[:, 0:MT])
            dump("cc", cbuf[:, 0:MT])
            dump("draw", yrow[:, 0:MT])
        ser.s(lambda e: e.activation(sprow[:, :], yrow[:, :], AF.Abs))
        ser.s(lambda e: e.activation(sprow[:, :], sprow[:, :], AF.Exp, scale=-1.0))
        ser.s(lambda e: e.activation(sprow[:, :], sprow[:, :], AF.Ln, bias=1.0))
        ser.v(lambda e: e.tensor_scalar_max(delta[:, :], yrow[:, :], 0.0))
        ser.v(lambda e: e.tensor_tensor(delta[:, :], delta[:, :], sprow[:, :], ALU.add))
        for tt in range(NTT):
            sl = slice(tt * 512, (tt + 1) * 512)
            ser.t(lambda e, sl=sl: e.matmul(ps16[:, :], negA[:, :], delta[:, sl],
                                            start=True, stop=True))
            ser.s(lambda e, sl=sl: e.activation(dA[:, sl], ps16[:, :], AF.Exp))
            ser.t(lambda e, sl=sl: e.matmul(ps16[:, :], ones_1p[:, 0:16], delta[:, sl],
                                            start=True, stop=True))
            ser.v(lambda e, sl=sl: e.tensor_tensor(dBx[:, sl], dBx[:, sl],
                                                   ps16[:, :], ALU.mult))
            ser.v(lambda e, sl=sl: e.tensor_tensor(
                dBx[:, sl], dBx[:, sl], xc[0:16, 0, 3 + sl.start:3 + sl.start + 512],
                ALU.mult))
        if mt == 0:
            dump("delta", delta[:, 0:MT])
            dump("dA", dA[:, 0:MT])
            dump("dBx", dBx[:, 0:MT])
            ser.v(lambda e: e.memset(hcol[:, :], 0.0))
        ser.v(lambda e: e.tensor_tensor_scan(hseq[:, :], dA[:, :], dBx[:, :],
                                             hcol[:, :], ALU.mult, ALU.add))
        if mt == 0:
            dump("hseq", hseq[:, 0:MT])
        ser.v(lambda e: e.tensor_copy(hcol[:, :], hseq[:, MT - 1:MT]))
        ser.v(lambda e: e.tensor_tensor(hseq[:, :], hseq[:, :], cbuf[:, :], ALU.mult))
        for tt in range(NTT):
            sl = slice(tt * 512, (tt + 1) * 512)
            ser.t(lambda e, sl=sl: e.matmul(psr0[:, :], ones16[:, :], hseq[:, sl],
                                            start=True, stop=True))
            ser.s(lambda e, sl=sl: e.copy(yrow[:, sl], psr0[:, :]))

        if mt == 0:
            dump("yrow", yrow[:, 0:MT])
        for c in range(16):
            for tt in range(NTT):
                sl = slice(tt * 512, (tt + 1) * 512)
                ser.t(lambda e, sl=sl: e.matmul(psb0[:, :], ones_1p[:, :], yrow[:, sl],
                                                start=True, stop=True))
                ser.s(lambda e, c=c, sl=sl: e.activation(zsil[:, :], z[:, c, sl], AF.Silu))
                ser.v(lambda e, c=c, sl=sl: e.scalar_tensor_tensor(
                    convacc[:, sl], xc[:, c, 3 + sl.start:3 + sl.start + 512],
                    dsb[:, c:c + 1], psb0[:, :], ALU.mult, ALU.add))
                ser.v(lambda e, c=c, sl=sl: e.tensor_tensor(
                    z[:, c, sl], convacc[:, sl], zsil[:, :], ALU.mult))

        if mt == 0:
            dump("yf0", z[:, 0, 0:MT])
        stream_loop(
            ser, NCH, NTT, 16,
            lambda oc, par: (lambda e: e.dma_start(
                out=wobuf[:, par, :, :],
                in_=w_out_t[:, oc * 128:(oc + 1) * 128].rearrange(
                    "(c p) f -> p c f", p=128))),
            lambda oc, tt, kc, par, bank: (lambda e: e.matmul(
                C.ps[bank][:, :], wobuf[:, par, kc, :],
                z[:, kc, tt * 512:(tt + 1) * 512],
                start=(kc == 0), stop=(kc == 15))),
            lambda oc, tt, bank: [("vector", lambda e: e.tensor_tensor(
                res[:, oc, tt * 512:(tt + 1) * 512], C.ps[bank][:, :],
                xbf[:, oc, tt * 512:(tt + 1) * 512], ALU.add))],
            enabled=PIPE["A"])

        if mt == 0:
            dump("res0", res[:, 0, 0:MT])
        for tt in range(NTT):
            sl = slice(tt * 512, (tt + 1) * 512)
            for oc in range(NCH):
                ser.t(lambda e, oc=oc, sl=sl: e.matmul(
                    psr0[:, :], ones128[:, :], res[:, oc, sl],
                    start=(oc == 0), stop=(oc == NCH - 1)))
            for oc in range(NCH):
                ser.s(lambda e, oc=oc, sl=sl: e.activation(
                    convacc[:, sl], res[:, oc, sl], AF.Square))
                ser.t(lambda e, oc=oc, sl=sl: e.matmul(
                    psr1[:, :], ones128[:, :], convacc[:, sl],
                    start=(oc == 0), stop=(oc == NCH - 1)))
            ser.s(lambda e, sl=sl: e.copy(srow[:, sl], psr0[:, :]))
            ser.s(lambda e, sl=sl: e.copy(sqrow[:, sl], psr1[:, :]))
        ser.v(lambda e: e.tensor_scalar_mul(murow[:, :], srow[:, :], 1.0 / DM))
        ser.v(lambda e: e.tensor_scalar(sqrow[:, :], sqrow[:, :], 1.0 / DM, 1e-5,
                                        ALU.mult, ALU.add))
        ser.v(lambda e: e.tensor_tensor(rstd, murow[:, :], murow[:, :], ALU.mult))
        ser.v(lambda e: e.tensor_tensor(rstd, sqrow[:, :], rstd, ALU.subtract))
        ser.s(lambda e: e.activation(rstd, rstd, AF.Ln))
        ser.s(lambda e: e.activation(rstd, rstd, AF.Exp, scale=-0.5))
        ser.v(lambda e: e.scalar_tensor_tensor(nmur, murow[:, :], -1.0,
                                               rstd, ALU.mult, ALU.mult))
        for tt in range(NTT):
            sl = slice(tt * 512, (tt + 1) * 512)
            ser.t(lambda e, sl=sl: e.matmul(psb0[:, :], ones_1p[:, :],
                                            dA[0:1, sl], start=True, stop=True))
            ser.t(lambda e, sl=sl: e.matmul(psb1[:, :], ones_1p[:, :],
                                            dBx[0:1, sl], start=True, stop=True))
            for oc in range(NCH):
                ser.v(lambda e, oc=oc, sl=sl: e.tensor_tensor(
                    res[:, oc, sl], res[:, oc, sl], psb0[:, :], ALU.mult))
                ser.v(lambda e, oc=oc, sl=sl: e.tensor_tensor(
                    res[:, oc, sl], res[:, oc, sl], psb1[:, :], ALU.add))
        ser.dmas(lambda e, t0=t0: e.dma_start(
            out=out_fm.rearrange("(c p) t -> p c t", p=128)[:, :, t0:t0 + MT],
            in_=res[:, :, :]))



def build_l0_ffn(ser, nc, C, dram, dbg=None):
    dbg = dbg or {}
    def dump(name, buf):
        if name in dbg:
            h = dbg[name]
            ser.dmas(lambda e, h=h, buf=buf: e.dma_start(out=h[:, :], in_=buf))

    MT = 1024
    NMT = T // MT
    NTT = MT // 512

    al = nc.alloc_sbuf_tensor
    xbf = al("B_xbf", [128, NCH, MT], BF16)
    wf = al("B_wf", [128, 2, 32, 128], F32)
    wt = al("B_wt", [128, 2, 32, 128], BF16)
    wt2 = al("B_wt2", [128, 2, 32, 128], BF16)
    h = al("B_h", [128, 32, MT], BF16)
    res = al("B_res", [128, NCH, MT], F32)
    b1sb = al("B_b1", [128, 32], F32)
    b2sb = al("B_b2", [128, NCH], F32)
    colsums = al("B_colsums", [128, 32], F32)
    Bsq2 = al("B_sq2", [128, 2, 512], F32)
    thr = al("B_thr", [128, 2, 2], F32)
    scal = al("B_scal", [1, 2], F32)

    x_fm, out_fm = dram["x_fm"], dram["out_fm"]
    w1t, w2t = dram["w1t"], dram["w2t"]

    ser.dmas(lambda e: e.dma_start(out=b1sb[:, :], in_=dram["b1_s"][:, :]))
    ser.dmas(lambda e: e.dma_start(out=b2sb[:, :], in_=dram["b2_s"][:, :]))

    for wi, (wd, ncol) in enumerate([(w1t, 4096), (w2t, 1024)]):
        nsl = ncol // 128
        for oc in range(nsl):
            wsl = wd[:, oc * 128:(oc + 1) * 128]
            nkc = wd.shape[0] // 128
            ser.dmas(lambda e, wsl=wsl, nkc=nkc: e.dma_start(
                out=wf[:, 0, 0:nkc, :], in_=wsl.rearrange("(c p) f -> p c f", p=128)))
            ser.v(lambda e, oc=oc, nkc=nkc: e.reduce_sum(
                colsums[:, oc:oc + 1], wf[:, 0, 0:nkc, :], AX.XY,
                apply_absolute_value=True))
        ser.v(lambda e, nsl=nsl: e.reduce_sum(colsums[:, 0:1], colsums[:, 0:nsl], AX.X))
        ser.t(lambda e: e.matmul(C.ps[0][0:1, 0:1], C.ones128[:, :], colsums[:, 0:1],
                                 start=True, stop=True))
        ser.s(lambda e, wi=wi: e.copy(scal[:, wi:wi + 1], C.ps[0][0:1, 0:1]))
        ser.v(lambda e, wi=wi: e.tensor_scalar_mul(scal[:, wi:wi + 1], scal[:, wi:wi + 1],
                                                   1.0 / (4096.0 * 1024.0)))
        ser.v(lambda e, wi=wi: e.tensor_scalar_max(scal[:, wi:wi + 1], scal[:, wi:wi + 1],
                                                   1e-8))
        ser.v(lambda e, wi=wi: e.tensor_scalar_mul(scal[:, wi:wi + 1], scal[:, wi:wi + 1],
                                                   0.5))
        ser.t(lambda e, wi=wi: e.matmul(C.ps[0][:, 0:1], C.ones_1p[:, :],
                                        scal[:, wi:wi + 1], start=True, stop=True))
        ser.s(lambda e, wi=wi: e.copy(thr[:, wi, 0:1], C.ps[0][:, 0:1]))
        ser.v(lambda e, wi=wi: e.tensor_scalar_mul(thr[:, wi, 1:2], thr[:, wi, 0:1], -1.0))

    def ternarize(nkc, wi, par):
        ser.v(lambda e, nkc=nkc, wi=wi, par=par: e.tensor_scalar(
            wt[:, par, 0:nkc, :], wf[:, par, 0:nkc, :], thr[:, wi, 0:1], None,
            ALU.is_gt))
        ser.v(lambda e, nkc=nkc, wi=wi, par=par: e.tensor_scalar(
            wt2[:, par, 0:nkc, :], wf[:, par, 0:nkc, :], thr[:, wi, 1:2], None,
            ALU.is_lt))
        ser.v(lambda e, nkc=nkc, par=par: e.tensor_tensor(
            wt[:, par, 0:nkc, :], wt[:, par, 0:nkc, :], wt2[:, par, 0:nkc, :],
            ALU.subtract))

    for mt in range(NMT):
        t0 = mt * MT
        ser.dmag(lambda e, t0=t0: e.dma_start(
            out=xbf[:, :, :],
            in_=x_fm.rearrange("(c p) t -> p c t", p=128)[:, :, t0:t0 + MT]))
        mm_last = {}
        evac_by_bank = {}
        for oc in range(32):
            par = oc % 2
            wsl = w1t[:, oc * 128:(oc + 1) * 128]
            ser.dmas(lambda e, wsl=wsl, par=par: e.dma_start(
                out=wf[:, par, 0:8, :],
                in_=wsl.rearrange("(c p) f -> p c f", p=128)),
                after=(mm_last.get(oc - 2) if PIPE["B"] else None))
            ternarize(8, 0, par)
            prep_end = ser.total
            for tt in range(NTT):
                bank = tt % 2
                deps = [prep_end]
                if bank in evac_by_bank:
                    deps.append(evac_by_bank[bank])
                for kc in range(8):
                    ser.t(lambda e, bank=bank, kc=kc, tt=tt, par=par: e.matmul(
                        C.ps[bank][:, :], wt[:, par, kc, :],
                        xbf[:, kc, tt * 512:(tt + 1) * 512],
                        start=(kc == 0), stop=(kc == 7)),
                        after=(max(deps) if (kc == 0 and PIPE["B"]) else None),
                        inc=(1 if kc == 7 else 0))
                mm_last[oc] = ser.total
                ser.s(lambda e, oc=oc, tt=tt, bank=bank: e.activation(
                    h[:, oc, tt * 512:(tt + 1) * 512], C.ps[bank][:, :], AF.Gelu,
                    bias=b1sb[:, oc:oc + 1]),
                    after=(mm_last[oc] if PIPE["B"] else None))
                evac_by_bank[bank] = ser.total
        if mt == 0:
            dump("h0", h[:, 0, 0:MT])
        mm_last = {}
        evac_by_bank = {}
        for oc in range(NCH):
            par = oc % 2
            wsl = w2t[:, oc * 128:(oc + 1) * 128]
            ser.dmas(lambda e, wsl=wsl, par=par: e.dma_start(
                out=wf[:, par, 0:32, :],
                in_=wsl.rearrange("(c p) f -> p c f", p=128)),
                after=(mm_last.get(oc - 2) if PIPE["B"] else None))
            ternarize(32, 1, par)
            prep_end = ser.total
            for tt in range(NTT):
                bank = tt % 2
                deps = [prep_end]
                if bank in evac_by_bank:
                    deps.append(evac_by_bank[bank])
                for kc in range(32):
                    ser.t(lambda e, bank=bank, kc=kc, tt=tt, par=par: e.matmul(
                        C.ps[bank][:, :], wt[:, par, kc, :],
                        h[:, kc, tt * 512:(tt + 1) * 512],
                        start=(kc == 0), stop=(kc == 31)),
                        after=(max(deps) if (kc == 0 and PIPE["B"]) else None),
                        inc=(1 if kc == 31 else 0))
                mm_last[oc] = ser.total
                ser.s(lambda e, oc=oc, tt=tt, bank=bank: e.activation(
                    Bsq2[:, bank, :], C.ps[bank][:, :], AF.Identity,
                    bias=b2sb[:, oc:oc + 1]),
                    after=(mm_last[oc] if PIPE["B"] else None))
                ser.v(lambda e, oc=oc, tt=tt, bank=bank: e.tensor_tensor(
                    res[:, oc, tt * 512:(tt + 1) * 512],
                    Bsq2[:, bank, :],
                    xbf[:, oc, tt * 512:(tt + 1) * 512], ALU.add))
                evac_by_bank[bank] = ser.total
        if mt == 0:
            dump("res0", res[:, 0, 0:MT])
        emit_ln(ser, C, res, NCH, MT,
                out_dma=lambda e, t0=t0: e.dma_start(
                    out=out_fm.rearrange("(c p) t -> p c t", p=128)[:, :, t0:t0 + MT],
                    in_=res[:, :, :]))



def build_l1_attn(ser, nc, C, dram, dbg=None):
    dbg = dbg or {}
    def dump(name, buf, shape2=None):
        if name in dbg:
            h = dbg[name]
            ser.dmas(lambda e, h=h, buf=buf: e.dma_start(out=h[:, :], in_=buf))

    MT = 1024
    NMT = T // MT
    NTT = MT // 512
    NCK = MT // 128
    H = 8

    al = nc.alloc_sbuf_tensor
    xbf = al("D_xbf", [128, NCH, MT], BF16)
    wwr = al("D_wwr", [128, 8, 1024], BF16)
    wrd = al("D_wrd", [128, 8, 1024], BF16)
    waugb = al("D_waug", [128, 8, 72], BF16)
    rk = al("D_rk", [128, H, 1 + MT], BF16)
    rkcar = al("D_rkcar", [128, H, 1], BF16)
    vsc = al("D_vsc", [128, NCK, 1024], BF16)
    ofm = al("D_ofm", [128, H, MT], BF16)
    res = al("D_res", [128, NCH, MT], F32)
    cumr = al("D_cumr", [8, MT], F32)
    lnbr = al("D_lnbr", [8, MT], F32)
    cendbr = al("D_cendbr", [8, MT], F32)
    betar = al("D_betar", [8, MT], F32)
    tailr = al("D_tailr", [8, MT], F32)
    er = al("D_er", [8, MT], F32)
    m1r = al("D_m1r", [8, MT], F32)
    colbuf = al("D_colbuf", [128, NCK, 40], F32)
    rows2 = al("D_rows2", [1, 2, MT], BF16)
    ones1pb = al("D_ones1pb", [1, 128], BF16)
    a0 = al("D_a0", [8, MT], F32)
    sp = al("D_sp", [8, MT], F32)
    bp = al("D_bp", [8, MT], F32)
    zero8 = al("D_zero8", [8, 128], F32)
    negA8 = al("D_negA8", [8, 1], F32)
    dtb = al("D_dtb", [8, 1], F32)
    trilneg = al("D_trilneg", [128, 128], F32)
    triu1 = al("D_triu1", [128, 128], F32)
    ident = al("D_ident", [128, 128], F32)
    identbf = al("D_identbf", [128, 128], BF16)
    ones2d = al("D_ones2d", [128, 128], F32)
    DIFF = al("D_DIFF", [128, 128], F32)
    DIFFN = al("D_DIFFN", [128, 128], F32)
    e1 = al("D_e1", [128, 128], F32)
    Mt = al("D_M", [128, 128], F32)
    intraT = al("D_intraT", [128, 128], F32)
    Pb = al("D_P", [128, 128], F32)
    Qb = al("D_Q", [128, 128], F32)
    Bb = al("D_B", [128, 128], F32)
    wsdneg = al("D_wsdneg", [128, 128], F32)
    rkd = al("D_rkd", [128, 128], F32)
    wkt = al("D_wkt", [128, 128], F32)
    rhs0 = al("D_rhs0", [128, 128], F32)
    vnew = al("D_vnew", [128, 128], F32)
    osb = al("D_osb", [128, 128], BF16)
    cedc = al("D_cedc", [128, 1], F32)
    S = al("D_S", [128, H, 128], F32)

    psT = C.psbf

    x_fm, out_fm = dram["x_fm"], dram["out_fm"]
    rowsd = dram["rows_scratch"]

    ser.dmag(lambda e: e.dma_start(
        out=wwr[:, :, :], in_=dram["w_write_t"].rearrange("(c p) f -> p c f", p=128)))
    ser.dmag(lambda e: e.dma_start(
        out=wrd[:, :, :], in_=dram["w_read_t"].rearrange("(c p) f -> p c f", p=128)))
    ser.dmag(lambda e: e.dma_start(
        out=waugb[:, :, :], in_=dram["waug"].rearrange("(c p) f -> p c f", p=128)))
    ser.dmag(lambda e: e.dma_start(out=identbf[:, :], in_=dram["ident"][:, :]))
    ser.dmas(lambda e: e.dma_start(out=negA8[:, :], in_=dram["alog8"][:, :]))
    ser.dmas(lambda e: e.dma_start(out=dtb[:, :], in_=dram["dtb8"][:, :]))
    ser.dmas(lambda e: e.dma_start(out=trilneg[:, :], in_=dram["trilneg"][:, :]))
    ser.dmas(lambda e: e.dma_start(out=triu1[:, :], in_=dram["triu1"][:, :]))
    ser.dmas(lambda e: e.dma_start(out=ident[:, :], in_=dram["ident"][:, :]))
    ser.dmas(lambda e: e.dma_start(out=ones2d[:, :], in_=dram["ones2d"][:, :]))
    ser.s(lambda e: e.activation(negA8[:, :], negA8[:, :], AF.Exp))
    ser.v(lambda e: e.tensor_scalar_mul(negA8[:, :], negA8[:, :], -1.0))
    ser.v(lambda e: e.memset(zero8[:, :], 0.0))
    ser.dmag(lambda e: e.dma_start(out=ones1pb[:, :], in_=dram["ones_1p"][:, :]))
    ser.v(lambda e: e.memset(rk[:, :, 0:1], 0.0))
    ser.v(lambda e: e.memset(S[:, :, :], 0.0))

    for mt in range(NMT):
        t0 = mt * MT
        ser.dmag(lambda e, t0=t0: e.dma_start(
            out=xbf[:, :, :],
            in_=x_fm.rearrange("(c p) t -> p c t", p=128)[:, :, t0:t0 + MT]))

        for h in range(H):
            for tt in range(NTT):
                sl = slice(tt * 512, (tt + 1) * 512)
                ser.s(lambda e, h=h, sl=sl: e.activation(
                    C.sq[:, 0:512], xbf[:, h, sl], AF.Square))
                ser.t(lambda e, sl=sl: e.matmul(
                    C.ps[3][0:1, :], C.ones128[:, :], C.sq[:, 0:512],
                    start=True, stop=True))
                ser.s(lambda e, sl=sl: e.copy(C.r0[:, sl], C.ps[3][0:1, :]))
            ser.v(lambda e: e.tensor_scalar_max(C.r0[:, 0:MT], C.r0[:, 0:MT], 1e-24))
            ser.s(lambda e: e.activation(C.r0[:, 0:MT], C.r0[:, 0:MT], AF.Ln))
            ser.s(lambda e: e.activation(C.r0[:, 0:MT], C.r0[:, 0:MT], AF.Exp,
                                         scale=-0.5))
            for tt in range(NTT):
                sl = slice(tt * 512, (tt + 1) * 512)
                ser.t(lambda e, sl=sl: e.matmul(
                    C.ps[5][:, :], C.ones_1p[:, :], C.r0[:, sl],
                    start=True, stop=True))
                ser.v(lambda e, h=h, sl=sl: e.tensor_tensor(
                    rk[:, h, 1 + sl.start:1 + sl.start + 512], xbf[:, h, sl],
                    C.ps[5][:, :], ALU.mult))

        for tt in range(NTT):
            sl = slice(tt * 512, (tt + 1) * 512)
            for kc in range(8):
                ser.t(lambda e, kc=kc, sl=sl: e.matmul(
                    C.ps[0][0:72, :], waugb[:, kc, :], xbf[:, kc, sl],
                    start=(kc == 0), stop=(kc == 7)))
            ser.s(lambda e, sl=sl: e.activation(
                a0[:, sl], C.ps[0][0:8, :], AF.Identity, bias=dtb[:, 0:1]))
            ser.s(lambda e, sl=sl: e.copy(bp[:, sl], C.ps[0][64:72, :]))
        ser.s(lambda e: e.activation(sp[:, :], a0[:, :], AF.Abs))
        ser.s(lambda e: e.activation(sp[:, :], sp[:, :], AF.Exp, scale=-1.0))
        ser.s(lambda e: e.activation(sp[:, :], sp[:, :], AF.Ln, bias=1.0))
        ser.v(lambda e: e.tensor_scalar_max(a0[:, :], a0[:, :], 0.0))
        ser.v(lambda e: e.tensor_tensor(a0[:, :], a0[:, :], sp[:, :], ALU.add))
        ser.v(lambda e: e.tensor_scalar(a0[:, :], a0[:, :], negA8[:, 0:1], None,
                                        ALU.mult))
        for ck in range(NCK):
            csl = slice(ck * 128, (ck + 1) * 128)
            ser.v(lambda e, csl=csl: e.tensor_tensor_scan(
                cumr[:, csl], a0[:, csl], zero8[:, :], 0.0, ALU.add, ALU.add))
        ser.s(lambda e: e.activation(sp[:, :], bp[:, :], AF.Abs))
        ser.s(lambda e: e.activation(sp[:, :], sp[:, :], AF.Exp, scale=-1.0))
        ser.s(lambda e: e.activation(sp[:, :], sp[:, :], AF.Ln, bias=1.0))
        ser.v(lambda e: e.tensor_scalar_max(a0[:, :], bp[:, :], 0.0))
        ser.v(lambda e: e.tensor_tensor(a0[:, :], bp[:, :], a0[:, :], ALU.subtract))
        ser.v(lambda e: e.tensor_tensor(lnbr[:, :], a0[:, :], sp[:, :],
                                        ALU.subtract))
        ser.s(lambda e: e.activation(betar[:, :], bp[:, :], AF.Sigmoid))
        for ck in range(NCK):
            csl = slice(ck * 128, (ck + 1) * 128)
            ser.v(lambda e, ck=ck, csl=csl: e.tensor_scalar(
                cendbr[:, csl], zero8[:, :],
                cumr[0:8, ck * 128 + 127:ck * 128 + 128], 1.0,
                ALU.add, ALU.mult))
        ser.v(lambda e: e.tensor_tensor(tailr[:, :], cendbr[:, :], cumr[:, :],
                                        ALU.subtract))
        ser.s(lambda e: e.activation(tailr[:, :], tailr[:, :], AF.Exp))
        ser.s(lambda e: e.activation(er[:, :], cumr[:, :], AF.Exp))
        ser.v(lambda e: e.scalar_tensor_tensor(m1r[:, :], betar[:, :], -1.0,
                                               er[:, :], ALU.mult, ALU.mult))
        ser.dmas(lambda e: e.dma_start(out=rowsd[0, :, :], in_=m1r[:, :]))
        ser.dmas(lambda e: e.dma_start(out=rowsd[1, :, :], in_=er[:, :]))
        for qi, qt in enumerate([cumr, lnbr, cendbr, betar, tailr]):
            for ck in range(NCK):
                csl = slice(ck * 128, (ck + 1) * 128)
                ser.t(lambda e, qt=qt, csl=csl: e.transpose(
                    C.ps[2][0:128, 0:8], qt[:, csl], ident[0:8, 0:8]))
                ser.s(lambda e, ck=ck, qi=qi: e.copy(
                    colbuf[:, ck, qi * 8:(qi + 1) * 8], C.ps[2][0:128, 0:8]))
        ser.s(lambda e: e.activation(colbuf[:, :, 16:24], colbuf[:, :, 16:24], AF.Exp))

        for tk in range(NCK):
            for vb in range(2):
                psd = C.ps[vb]
                for kc in range(8):
                    ser.t(lambda e, psd=psd, kc=kc, tk=tk, vb=vb: e.matmul(
                        psd[:, :], xbf[:, kc, tk * 128:(tk + 1) * 128],
                        wwr[:, kc, vb * 512:(vb + 1) * 512],
                        start=(kc == 0), stop=(kc == 7)))
                for hh in range(4):
                    h = vb * 4 + hh
                    ser.v(lambda e, psd=psd, tk=tk, vb=vb, hh=hh, h=h: e.tensor_scalar(
                        vsc[:, tk, h * 128:(h + 1) * 128],
                        psd[:, hh * 128:(hh + 1) * 128],
                        colbuf[:, tk, 24 + h:25 + h], None, ALU.mult))

        for h in range(H):
            ser.dmag(lambda e, h=h: e.dma_start(out=rows2[:, 0, :], in_=rowsd[0, h:h + 1, :]))
            ser.dmag(lambda e, h=h: e.dma_start(out=rows2[:, 1, :], in_=rowsd[1, h:h + 1, :]))
            for ck in range(NCK):
                rks = rk[:, h, 1 + ck * 128:1 + ck * 128 + 128]
                wks = rk[:, h, ck * 128:ck * 128 + 128]
                ccol = colbuf[:, ck, h:h + 1]
                lnbcol = colbuf[:, ck, 8 + h:9 + h]
                cedcol = colbuf[:, ck, 16 + h:17 + h]
                tailcol = colbuf[:, ck, 32 + h:33 + h]
                m1row = rows2[:, 0, ck * 128:(ck + 1) * 128]
                erow = rows2[:, 1, ck * 128:(ck + 1) * 128]
                Ssl = S[:, h, :]

                ser.v(lambda e, ccol=ccol: e.tensor_scalar(
                    DIFF[:, :], ones2d[:, :], ccol, None, ALU.mult))
                ser.t(lambda e: e.transpose(C.ps[2][:, 0:128], DIFF[:, :], ident[:, :]))
                ser.v(lambda e: e.scalar_tensor_tensor(
                    DIFF[:, :], C.ps[2][:, 0:128], -1.0, DIFF[:, :],
                    ALU.mult, ALU.add))
                ser.v(lambda e: e.tensor_scalar(
                    DIFFN[:, :], DIFF[:, :], -1.0, 0.0, ALU.mult, ALU.min))
                ser.v(lambda e: e.tensor_scalar_min(DIFF[:, :], DIFF[:, :], 0.0))
                ser.s(lambda e, lnbcol=lnbcol: e.activation(
                    e1[:, :], DIFF[:, :], AF.Exp, bias=lnbcol))
                ser.s(lambda e: e.activation(DIFFN[:, :], DIFFN[:, :], AF.Exp))
                ser.t(lambda e, wks=wks: e.matmul(C.ps[0][:, 0:128], wks, wks,
                                                  start=True, stop=True))
                ser.v(lambda e: e.tensor_tensor(Mt[:, :], C.ps[0][:, 0:128],
                                                e1[:, :], ALU.mult))
                ser.v(lambda e: e.tensor_tensor(Mt[:, :], Mt[:, :], trilneg[:, :],
                                                ALU.mult))
                ser.t(lambda e, wks=wks, rks=rks: e.matmul(
                    C.ps[1][:, 0:128], wks, rks, start=True, stop=True))
                ser.v(lambda e: e.tensor_tensor(intraT[:, :], C.ps[1][:, 0:128],
                                                DIFFN[:, :], ALU.mult))
                ser.v(lambda e: e.tensor_tensor(intraT[:, :], intraT[:, :],
                                                triu1[:, :], ALU.mult))
                ser.t(lambda e: e.transpose(C.ps[2][:, 0:128], Mt[:, :], ident[:, :]))
                ser.v(lambda e: e.tensor_copy(Pb[:, :], C.ps[2][:, 0:128]))
                ser.v(lambda e: e.tensor_tensor(Bb[:, :], C.ps[2][:, 0:128],
                                                ident[:, :], ALU.add))
                ser.v(lambda e: e.tensor_copy(Qb[:, :], Mt[:, :]))
                for lvl in range(6):
                    ser.t(lambda e: e.matmul(C.ps[1][:, 0:128], Pb[:, :], Qb[:, :],
                                             start=True, stop=True))
                    ser.t(lambda e: e.matmul(C.ps[2][:, 0:128], Qb[:, :], Pb[:, :],
                                             start=True, stop=True))
                    ser.v(lambda e: e.tensor_copy(Qb[:, :], C.ps[1][:, 0:128]))
                    ser.v(lambda e: e.tensor_copy(Pb[:, :], C.ps[2][:, 0:128]))
                    ser.t(lambda e: e.matmul(C.ps[0][:, 0:128], Qb[:, :], Bb[:, :],
                                             start=True, stop=True))
                    ser.v(lambda e: e.tensor_tensor(Bb[:, :], Bb[:, :],
                                                    C.ps[0][:, 0:128], ALU.add))
                ser.t(lambda e, m1row=m1row: e.matmul(
                    C.ps[0][:, 0:128], ones1pb[:, :], m1row, start=True, stop=True))
                ser.v(lambda e, wks=wks: e.tensor_tensor(
                    wsdneg[:, :], wks, C.ps[0][:, 0:128], ALU.mult))
                ser.t(lambda e, erow=erow: e.matmul(
                    C.ps[1][:, 0:128], ones1pb[:, :], erow, start=True, stop=True))
                ser.v(lambda e, rks=rks: e.tensor_tensor(
                    rkd[:, :], rks, C.ps[1][:, 0:128], ALU.mult))
                ser.t(lambda e, wks=wks: e.transpose(psT[:, 0:128], wks, identbf[:, :]))
                ser.v(lambda e, tailcol=tailcol: e.tensor_scalar(
                    wkt[:, :], psT[:, 0:128], tailcol, None, ALU.mult))
                ser.t(lambda e: e.matmul(C.ps[0][:, 0:128], wsdneg[:, :], Ssl,
                                         start=True, stop=True))
                ser.v(lambda e, ck=ck, h=h: e.tensor_tensor(
                    rhs0[:, :], C.ps[0][:, 0:128], vsc[:, ck, h * 128:(h + 1) * 128],
                    ALU.add))
                ser.t(lambda e: e.matmul(C.ps[1][:, 0:128], Bb[:, :], rhs0[:, :],
                                         start=True, stop=True))
                ser.v(lambda e: e.tensor_copy(vnew[:, :], C.ps[1][:, 0:128]))
                ser.t(lambda e: e.matmul(C.ps[0][:, 0:128], rkd[:, :], Ssl,
                                         start=True, stop=False))
                ser.t(lambda e: e.matmul(C.ps[0][:, 0:128], intraT[:, :], vnew[:, :],
                                         start=False, stop=True))
                ser.v(lambda e: e.tensor_copy(osb[:, :], C.ps[0][:, 0:128]))
                ser.t(lambda e: e.matmul(C.ps[1][:, 0:128], wkt[:, :], vnew[:, :],
                                         start=True, stop=True))
                ser.s(lambda e, cedcol=cedcol: e.copy(cedc[:, :], cedcol))
                ser.v(lambda e, Ssl=Ssl: e.scalar_tensor_tensor(
                    Ssl, Ssl, cedc[:, 0:1], C.ps[1][:, 0:128], ALU.mult, ALU.add))
                ser.t(lambda e: e.transpose(psT[:, 0:128], osb[:, :], identbf[:, :]))
                ser.s(lambda e, h=h, ck=ck: e.copy(
                    ofm[:, h, ck * 128:(ck + 1) * 128], psT[:, 0:128]))

        if mt < NMT - 1:
            ser.v(lambda e: e.tensor_copy(rkcar[:, :, :], rk[:, :, MT:MT + 1]))
            ser.v(lambda e: e.tensor_copy(rk[:, :, 0:1], rkcar[:, :, :]))

        for oc in range(NCH):
            for tt in range(NTT):
                sl = slice(tt * 512, (tt + 1) * 512)
                psd = C.ps[tt % 2]
                for kc in range(8):
                    ser.t(lambda e, psd=psd, kc=kc, oc=oc, sl=sl: e.matmul(
                        psd[:, :], wrd[:, kc, oc * 128:(oc + 1) * 128],
                        ofm[:, kc, sl], start=(kc == 0), stop=(kc == 7)))
                ser.v(lambda e, oc=oc, sl=sl, psd=psd: e.scalar_tensor_tensor(
                    res[:, oc, sl], xbf[:, oc, sl], 2.0, psd[:, :],
                    ALU.mult, ALU.add))
        if mt == 0:
            dump("res0", res[:, 0, 0:MT])
            dump("ofm0", ofm[:, 0, 0:MT])
        emit_ln(ser, C, res, NCH, MT,
                out_dma=lambda e, t0=t0: e.dma_start(
                    out=out_fm.rearrange("(c p) t -> p c t", p=128)[:, :, t0:t0 + MT],
                    in_=res[:, :, :]))



def build_l1_ffn(ser, nc, C, dram, layers, final_ln, dbg=None):
    dbg = dbg or {}
    def dump(name, buf):
        if name in dbg:
            h = dbg[name]
            ser.dmas(lambda e, h=h, buf=buf: e.dma_start(out=h[:, :], in_=buf))

    MT = 1024
    NMT = T // MT
    NTT = MT // 512
    NQT = MT // 128
    H16 = 16

    al = nc.alloc_sbuf_tensor
    xs = al("E_xs", [128, NCH, 1 + T], BF16)
    protT = al("E_protT", [128, 128], BF16)
    identbf = al("E_identbf", [128, 128], BF16)
    ones128b = al("E_ones128b", [128, 1], BF16)
    mstd = al("E_mstd", [128, 384], BF16)
    mq0 = al("E_mq0", [128, 128], BF16)
    mq1 = al("E_mq1", [128, 256], BF16)
    mstdT = al("E_mstdT", [128, 384], BF16)
    mq0T = al("E_mq0T", [128, 128], BF16)
    mq1T = al("E_mq1T", [128, 256], BF16)
    qs_sb = al("E_qs", [1, 6, 64], F32)
    ks_sb = al("E_ks", [1, 6, 64], F32)
    b1a_sb = al("E_b1a", [128, 6, 22], F32)
    b1g_sb = al("E_b1g", [128, 6, 22], F32)
    b2_sb = al("E_b2", [128, 6, 8], F32)
    lng_sb = al("E_lng", [128, 6, 8], F32)
    lnb_sb = al("E_lnb", [128, 6, 8], F32)
    wdb = al("E_wdb", [128, 2, 8, 128], BF16)
    wv = al("E_wv", [128, 8, 1024], BF16)
    qraw = al("E_qraw", [128, 8, MT], BF16)
    kraw = al("E_kraw", [128, 8, 1280], BF16)
    vtm = al("E_vtm", [128, 10, 1024], BF16)
    attno = al("E_attno", [128, 8, MT], BF16)
    qno = al("E_qno", [128, 512], BF16)
    t1b = al("E_t1b", [128, 512], BF16)
    halo = al("E_halo", [128, 8, 257], BF16)
    curcol = al("E_curcol", [128, 8, 1], BF16)
    aprev = al("E_aprev", [128, 8, 1], BF16)
    Pall = al("E_Pall", [128, 10, 384], BF16)
    scrap = nc.alloc_sbuf_tensor_at("E_scrap", [128, 384], BF16,
                                    offset=nc.lookup_mloc(Pall).addr)
    _ovl = nc.lookup_mloc(kraw).addr
    negmax = al("E_negmax", [128, 1], F32)
    sumcol = al("E_sumcol", [128, 1], F32)
    identf = al("E_identf", [128, 128], F32)
    dsq1 = al("E_dsq1", [128, 512], F32)
    rowbuf = al("E_rowbuf", [1, 8, 128], F32)
    hbuf = nc.alloc_sbuf_tensor_at("E_hbuf", [128, 22, MT], BF16, offset=_ovl)
    res = nc.alloc_sbuf_tensor_at("E_res", [128, NCH, MT], F32, offset=_ovl)
    wb22 = nc.alloc_sbuf_tensor_at("E_wb22", [128, 2, 22, 128], BF16,
                                   offset=_ovl + 22 * MT * 2)
    statc = nc.alloc_sbuf_tensor_at("E_statc", [128, 128], F32,
                                    offset=_ovl + 22 * MT * 2)
    tblq = nc.alloc_sbuf_tensor_at("E_tblq", [128, 2, 1024], BF16,
                                   offset=_ovl + 22 * MT * 2 + 512)
    tblk = nc.alloc_sbuf_tensor_at("E_tblk", [128, 2, 1280], BF16,
                                   offset=_ovl + 22 * MT * 2 + 512 + 4096)
    xn = qraw

    x_fm, out_fm = dram["x_fm"], dram["out_fm"]
    statd = dram["stat_scratch"]

    ser.v(lambda e: e.memset(xs[:, :, 0:1], 0.0))
    ser.dmag(lambda e: e.dma_start(
        out=xs[:, :, 1:1 + T], in_=x_fm.rearrange("(c p) t -> p c t", p=128)))
    for nm, buf in [("protT", protT), ("ident", identbf), ("mstd", mstd),
                    ("mq0", mq0), ("mq1", mq1), ("mstdT", mstdT),
                    ("mq0T", mq0T), ("mq1T", mq1T), ("ones128", ones128b)]:
        ser.dmag(lambda e, nm=nm, buf=buf: e.dma_start(out=buf[:, :], in_=dram[nm][:, :]))
    ser.dmas(lambda e: e.dma_start(out=identf[:, :], in_=dram["ident"][:, :]))
    for nm, buf in [("qs_s", qs_sb), ("ks_s", ks_sb), ("b1a_s", b1a_sb),
                    ("b1g_s", b1g_sb), ("b2_s", b2_sb), ("lng_s", lng_sb),
                    ("lnb_s", lnb_sb)]:
        ser.dmas(lambda e, nm=nm, buf=buf: e.dma_start(out=buf[:, :, :], in_=dram[nm][:, :, :]))

    def xs_sh(c, a, b):
        off = 1 if c < 4 else 0
        return xs[:, c, off + a: off + b]

    def ksrc_sh(c, mt, t0, lo, w):
        if mt == 0:
            return xs_sh(c, lo, lo + w)
        if lo < 256:
            assert lo + w <= 256
            off = 1 if c < 4 else 0
            return halo[:, c, off + lo: off + lo + w]
        return xs_sh(c, t0 + lo - 256, t0 + lo - 256 + w)

    for li in layers:
        wqkv = dram["wqkv"]
        for mt in range(NMT):
            t0 = mt * MT
            kstart = max(0, t0 - 256)
            klen = t0 + MT - kstart
            nkc = klen // 128
            ser.dmag(lambda e, t0=t0: e.dma_start(
                out=tblq[:, 0, 0:MT], in_=dram["csq8"][:, t0:t0 + MT]))
            ser.dmag(lambda e, t0=t0: e.dma_start(
                out=tblq[:, 1, 0:MT], in_=dram["ssq8"][:, t0:t0 + MT]))
            ser.dmag(lambda e, kstart=kstart, klen=klen: e.dma_start(
                out=tblk[:, 0, 0:klen], in_=dram["csk"][:, kstart:kstart + klen]))
            ser.dmag(lambda e, kstart=kstart, klen=klen: e.dma_start(
                out=tblk[:, 1, 0:klen], in_=dram["ssk"][:, kstart:kstart + klen]))

            if mt > 0:
                ser.v(lambda e, t0=t0: e.tensor_copy(curcol[:, :, :],
                                                     xs[:, :, t0:t0 + 1]))
                ser.v(lambda e, t0=t0: e.tensor_copy(xs[:, :, t0:t0 + 1],
                                                     halo[:, :, 256:257]))
            stream_loop(
                ser, 8, NTT, 8,
                lambda oc, par, li=li: (lambda e: e.dma_start(
                    out=wdb[:, par, :, :],
                    in_=wqkv[li, :, oc * 128:(oc + 1) * 128].rearrange(
                        "(c p) f -> p c f", p=128))),
                lambda oc, tt, kc, par, bank, t0=t0: (lambda e: e.matmul(
                    C.ps[bank][:, :], wdb[:, par, kc, :],
                    xs_sh(kc, t0 + tt * 512, t0 + (tt + 1) * 512),
                    start=(kc == 0), stop=(kc == 7))),
                lambda oc, tt, bank: [("scalar", lambda e: e.copy(
                    qraw[:, oc, tt * 512:(tt + 1) * 512], C.ps[bank][:, :]))],
                enabled=PIPE["Dq"])
            if mt == 0:
                kts = [(0, 512), (512, 512)]
            else:
                kts = [(0, 256), (256, 512), (768, 512)]
            stream_loop(
                ser, 8, len(kts), 8,
                lambda oc, par, li=li: (lambda e: e.dma_start(
                    out=wdb[:, par, :, :],
                    in_=wqkv[li, :, 1024 + oc * 128:1024 + (oc + 1) * 128].rearrange(
                        "(c p) f -> p c f", p=128))),
                lambda oc, tt, kc, par, bank, mt=mt, t0=t0, kts=kts: (
                    lambda e: e.matmul(
                        C.ps[bank][:, 0:kts[tt][1]], wdb[:, par, kc, :],
                        ksrc_sh(kc, mt, t0, kts[tt][0], kts[tt][1]),
                        start=(kc == 0), stop=(kc == 7))),
                lambda oc, tt, bank, kts=kts: [("scalar", lambda e: e.copy(
                    kraw[:, oc, kts[tt][0]:kts[tt][0] + kts[tt][1]],
                    C.ps[bank][:, 0:kts[tt][1]]))], enabled=PIPE["Dk"])
            ser.dmag(lambda e, li=li: e.dma_start(
                out=wv[:, :, :],
                in_=wqkv[li, :, 2048:3072].rearrange("(c p) f -> p c f", p=128)))
            for tk in range(nkc):
                for vb in range(2):
                    psd = C.ps[vb]
                    for kc in range(8):
                        ser.t(lambda e, psd=psd, kc=kc, tk=tk, vb=vb,
                              mt=mt, t0=t0: e.matmul(
                            psd[:, :],
                            ksrc_sh(kc, mt, t0, tk * 128, 128),
                            wv[:, kc, vb * 512:(vb + 1) * 512],
                            start=(kc == 0), stop=(kc == 7)))
                    ser.s(lambda e, tk=tk, vb=vb, psd=psd: e.copy(
                        vtm[:, tk, vb * 512:(vb + 1) * 512], psd[:, :]))
            if mt < NMT - 1:
                ser.v(lambda e, t0=t0: e.tensor_copy(
                    halo[:, :, :], xs[:, :, t0 + 768:t0 + 1025]))

            def norm_rope(buf, nlen, base, cs, ss, sc_sb, li):
                nt = [(s, min(512, nlen - s)) for s in range(0, nlen, 512)]
                for ch in range(8):
                    for (s, w) in nt:
                        ser.s(lambda e, ch=ch, s=s, w=w, buf=buf: e.activation(
                            C.sq[:, 0:w], buf[:, ch, s:s + w], AF.Square))
                        ser.t(lambda e, s=s, w=w: e.matmul(
                            C.ps[3][0:1, 0:w], C.ones128[0:64, :], C.sq[0:64, 0:w],
                            start=True, stop=True))
                        ser.t(lambda e, s=s, w=w: e.matmul(
                            C.ps[4][0:1, 0:w], C.ones128[64:128, :], C.sq[64:128, 0:w],
                            start=True, stop=True))
                        ser.s(lambda e, s=s, w=w: e.copy(C.r0[:, s:s + w], C.ps[3][0:1, 0:w]))
                        ser.s(lambda e, s=s, w=w: e.copy(C.r1[:, s:s + w], C.ps[4][0:1, 0:w]))
                    for rr in (C.r0, C.r1):
                        ser.v(lambda e, rr=rr, nlen=nlen: e.tensor_scalar_max(
                            rr[:, 0:nlen], rr[:, 0:nlen], 1e-24))
                        ser.s(lambda e, rr=rr, nlen=nlen: e.activation(
                            rr[:, 0:nlen], rr[:, 0:nlen], AF.Ln))
                        ser.s(lambda e, rr=rr, nlen=nlen: e.activation(
                            rr[:, 0:nlen], rr[:, 0:nlen], AF.Exp, scale=-0.5))
                    for (s, w) in nt:
                        ser.t(lambda e, s=s, w=w, li=li, sc_sb=sc_sb: e.matmul(
                            C.ps[5][0:64, 0:w], sc_sb[:, li, :], C.r0[:, s:s + w],
                            start=True, stop=True))
                        ser.t(lambda e, s=s, w=w, li=li, sc_sb=sc_sb: e.matmul(
                            C.ps[5][64:128, 0:w], sc_sb[:, li, :], C.r1[:, s:s + w],
                            start=True, stop=True, tile_position=(0, 64)))
                        ser.v(lambda e, ch=ch, s=s, w=w, buf=buf: e.tensor_tensor(
                            qno[:, 0:w], buf[:, ch, s:s + w], C.ps[5][:, 0:w],
                            ALU.mult))
                        ser.t(lambda e, s=s, w=w: e.matmul(
                            C.ps[6][:, 0:w], protT[:, :], qno[:, 0:w],
                            start=True, stop=True))
                        ser.v(lambda e, ch=ch, s=s, w=w, cs=cs: e.tensor_tensor(
                            t1b[:, 0:w], qno[:, 0:w], cs[:, s:s + w],
                            ALU.mult))
                        ser.v(lambda e, s=s, w=w, ss=ss: e.tensor_tensor(
                            qno[:, 0:w], C.ps[6][:, 0:w], ss[:, s:s + w],
                            ALU.mult))
                        ser.v(lambda e, ch=ch, s=s, w=w, buf=buf: e.tensor_tensor(
                            buf[:, ch, s:s + w], t1b[:, 0:w], qno[:, 0:w], ALU.add))
            norm_rope(qraw, MT, t0, tblq[:, 0], tblq[:, 1], qs_sb, li)
            norm_rope(kraw, klen, kstart, tblk[:, 0], tblk[:, 1], ks_sb, li)
            if mt == 0:
                dump("qpost", qraw[:, 0, 0:MT])
                dump("kpost", kraw[:, 0, 0:MT])
                dump("vt0", vtm[:, 0, 0:1024])

            for h in range(H16):
                pb = (h % 2) * 64
                chq = h // 2
                for qt in range(NQT):
                    if mt == 0 and qt == 0:
                        ks_, kw, msk = 0, 128, mq0
                    elif mt == 0 and qt == 1:
                        ks_, kw, msk = 0, 256, mq1
                    else:
                        ks_ = qt * 128 if mt > 0 else (qt - 2) * 128
                        kw, msk = 384, mstd
                    ser.t(lambda e, pb=pb, chq=chq, qt=qt, ks_=ks_, kw=kw: e.matmul(
                        C.ps[0][:, 0:kw],
                        qraw[pb:pb + 64, chq, qt * 128:(qt + 1) * 128],
                        kraw[pb:pb + 64, chq, ks_:ks_ + kw],
                        start=True, stop=False))
                    ser.t(lambda e, kw=kw, msk=msk: e.matmul(
                        C.ps[0][:, 0:kw], identbf[:, :], msk[:, 0:kw],
                        start=False, stop=True))
                    ser.v(lambda e, kw=kw: e.reduce_max(
                        negmax[:, :], C.ps[0][:, 0:kw], AX.X, negate=True))
                    ser.s(lambda e, kw=kw: e.activation(
                        scrap[:, 0:kw], C.ps[0][:, 0:kw], AF.Exp,
                        bias=negmax[:, 0:1], accum_out=sumcol[:, :]))
                    ser.s(lambda e: e.activation(sumcol[:, :], sumcol[:, :], AF.Ln))
                    ser.v(lambda e, h=h, qt=qt: e.tensor_tensor(
                        statc[:, qt * 16 + h:qt * 16 + h + 1], negmax[:, :],
                        sumcol[:, :], ALU.subtract))
            if mt == 0:
                dump("statc0", statc[:, :])
            if mt == 1:
                dump("statc1", statc[:, :])
            ser.t(lambda e: e.transpose(C.ps[2][:, 0:128], statc[:, :], identf[:, :]))
            ser.v(lambda e: e.tensor_copy(C.sq[:, 0:128], C.ps[2][:, 0:128]))
            ser.dmas(lambda e: e.dma_start(out=statd[:, :], in_=C.sq[:, 0:128]))

            for h in range(H16):
                pb = (h % 2) * 64
                chq = h // 2
                ser.dmas(lambda e, h=h: e.dma_start(
                    out=rowbuf[:, :, :],
                    in_=bass.AP(statd, h * 128, [[1, 1], [16 * 128, 8], [1, 128]])))
                spans = {}
                for kc in range(nkc):
                    qts = ([q for q in range(max(0, kc - 2), min(kc + 1, NQT))]
                           if mt > 0 else
                           [q for q in range(kc, min(kc + 3, NQT))])
                    spans[kc] = qts
                    if not qts:
                        continue
                    q0 = qts[0]
                    span = qts[-1] - q0 + 1
                    ser.t(lambda e, pb=pb, chq=chq, kc=kc, q0=q0, span=span: e.matmul(
                        C.ps[3][:, 0:span * 128],
                        kraw[pb:pb + 64, chq, kc * 128:(kc + 1) * 128],
                        qraw[pb:pb + 64, chq, q0 * 128:(q0 + span) * 128],
                        start=True, stop=False))
                    for q in qts:
                        if mt == 0 and q == 0:
                            mt_, mo = mq0T, 0
                        elif mt == 0 and q == 1:
                            mt_, mo = mq1T, kc * 128
                        else:
                            rel = (kc - q) if mt > 0 else (kc - q + 2)
                            mt_, mo = mstdT, (2 - rel) * 128
                        ser.t(lambda e, q=q, q0=q0, mt_=mt_, mo=mo: e.matmul(
                            C.ps[3][:, (q - q0) * 128:(q - q0 + 1) * 128],
                            identbf[:, :], mt_[:, mo:mo + 128],
                            start=False, stop=False))
                    ser.t(lambda e, q0=q0, span=span: e.matmul(
                        C.ps[3][:, 0:span * 128], C.ones_1p[:, :],
                        rowbuf[:, q0:q0 + span, :].rearrange("one a b -> one (a b)"),
                        start=False, stop=True))
                    ser.s(lambda e, span=span, kc=kc: e.activation(
                        Pall[:, kc, 0:span * 128], C.ps[3][:, 0:span * 128], AF.Exp))
                vsl0 = (h % 2) * 64 + (h // 2) * 128
                for q in range(NQT):
                    kcs = [kc for kc in range(nkc) if q in spans[kc]]
                    for j, kc in enumerate(kcs):
                        col = (q - spans[kc][0]) * 128
                        ser.t(lambda e, pb=pb, kc=kc, col=col, vsl0=vsl0,
                              st=(j == 0), sp=(j == len(kcs) - 1): e.matmul(
                            C.ps[0][pb:pb + 64, 0:128],
                            vtm[:, kc, vsl0:vsl0 + 64],
                            Pall[:, kc, col:col + 128],
                            start=st, stop=sp, tile_position=(0, pb)))
                    ser.s(lambda e, pb=pb, chq=chq, q=q: e.copy(
                        attno[pb:pb + 64, chq, q * 128:(q + 1) * 128],
                        C.ps[0][pb:pb + 64, 0:128]))

            if mt == 0:
                dump("attno0", attno[:, 0, 0:MT])
            if mt == 1:
                dump("attno1", attno[:, 0, 0:MT])
            stream_loop(
                ser, NCH, NTT, 8,
                lambda oc, par, li=li: (lambda e: e.dma_start(
                    out=wdb[:, par, :, :],
                    in_=dram["wo"][li, :, oc * 128:(oc + 1) * 128].rearrange(
                        "(c p) f -> p c f", p=128))),
                lambda oc, tt, kc, par, bank: (lambda e: e.matmul(
                    C.ps[bank][:, :], wdb[:, par, kc, :],
                    attno[:, kc, tt * 512:(tt + 1) * 512],
                    start=(kc == 0), stop=(kc == 7))),
                lambda oc, tt, bank, t0=t0: [("vector", lambda e: e.tensor_tensor(
                    xs[:, oc, 1 + t0 + tt * 512:1 + t0 + (tt + 1) * 512],
                    C.ps[bank][:, :],
                    xs[:, oc, 1 + t0 + tt * 512:1 + t0 + (tt + 1) * 512],
                    ALU.add))], enabled=PIPE["Dwo"])

            if mt == 0:
                dump("stream_a", xs[:, 0, 1:1 + MT])
            if mt > 0:
                ser.v(lambda e, t0=t0: e.tensor_copy(xs[:, :, t0:t0 + 1],
                                                     aprev[:, :, :]))
            if mt < NMT - 1:
                ser.v(lambda e, t0=t0: e.tensor_copy(
                    aprev[:, :, :], xs[:, :, t0 + MT:t0 + MT + 1]))
            for tt in range(NTT):
                sl = slice(tt * 512, (tt + 1) * 512)
                for oc in range(NCH):
                    ser.t(lambda e, oc=oc, tt=tt, t0=t0: e.matmul(
                        C.ps[3][0:1, :], ones128b[:, :],
                        xs_sh(oc, t0 + tt * 512, t0 + (tt + 1) * 512),
                        start=(oc == 0), stop=(oc == NCH - 1)))
                for oc in range(NCH):
                    ser.s(lambda e, oc=oc, tt=tt, t0=t0: e.activation(
                        C.sq[:, 0:512],
                        xs_sh(oc, t0 + tt * 512, t0 + (tt + 1) * 512), AF.Square))
                    ser.t(lambda e, oc=oc, tt=tt: e.matmul(
                        C.ps[4][0:1, :], C.ones128[:, :],
                        C.sq[:, 0:512],
                        start=(oc == 0), stop=(oc == NCH - 1)))
                ser.s(lambda e, sl=sl: e.copy(C.r0[:, sl], C.ps[3][0:1, :]))
                ser.s(lambda e, sl=sl: e.copy(C.r1[:, sl], C.ps[4][0:1, :]))
            ser.v(lambda e: e.tensor_scalar_mul(C.r0[:, 0:MT], C.r0[:, 0:MT],
                                                1.0 / DM))
            ser.v(lambda e: e.tensor_scalar(C.r1[:, 0:MT], C.r1[:, 0:MT],
                                            1.0 / DM, 1e-5, ALU.mult, ALU.add))
            ser.v(lambda e: e.tensor_tensor(C.r2[:, 0:MT], C.r0[:, 0:MT],
                                            C.r0[:, 0:MT], ALU.mult))
            ser.v(lambda e: e.tensor_tensor(C.r1[:, 0:MT], C.r1[:, 0:MT],
                                            C.r2[:, 0:MT], ALU.subtract))
            ser.s(lambda e: e.activation(C.r1[:, 0:MT], C.r1[:, 0:MT], AF.Ln))
            ser.s(lambda e: e.activation(C.r1[:, 0:MT], C.r1[:, 0:MT], AF.Exp,
                                         scale=-0.5))
            ser.v(lambda e: e.scalar_tensor_tensor(C.r2[:, 0:MT], C.r0[:, 0:MT],
                                                   -1.0, C.r1[:, 0:MT],
                                                   ALU.mult, ALU.mult))
            for tt in range(NTT):
                sl = slice(tt * 512, (tt + 1) * 512)
                ser.t(lambda e, sl=sl: e.matmul(C.ps[5][:, :], C.ones_1p[:, :],
                                                C.r1[:, sl], start=True, stop=True))
                ser.t(lambda e, sl=sl: e.matmul(C.ps[6][:, :], C.ones_1p[:, :],
                                                C.r2[:, sl], start=True, stop=True))
                for oc in range(NCH):
                    ser.v(lambda e, oc=oc, sl=sl, tt=tt, t0=t0: e.tensor_tensor(
                        xn[:, oc, sl],
                        xs_sh(oc, t0 + tt * 512, t0 + (tt + 1) * 512),
                        C.ps[5][:, :], ALU.mult))
                    ser.v(lambda e, oc=oc, sl=sl: e.tensor_tensor(
                        xn[:, oc, sl], xn[:, oc, sl], C.ps[6][:, :], ALU.add))
                    ser.v(lambda e, oc=oc, sl=sl, li=li: e.tensor_scalar(
                        xn[:, oc, sl], xn[:, oc, sl], lng_sb[:, li, oc:oc + 1],
                        lnb_sb[:, li, oc:oc + 1], ALU.mult, ALU.add))
            stream_loop(
                ser, 22, NTT, 8,
                lambda oc, par, li=li: (lambda e: e.dma_start(
                    out=wdb[:, par, :, :],
                    in_=dram["w1g"][li, :, oc * 128:(oc + 1) * 128].rearrange(
                        "(c p) f -> p c f", p=128))),
                lambda oc, tt, kc, par, bank: (lambda e: e.matmul(
                    C.ps[bank][:, :], wdb[:, par, kc, :],
                    xn[:, kc, tt * 512:(tt + 1) * 512],
                    start=(kc == 0), stop=(kc == 7))),
                lambda oc, tt, bank, li=li: [("scalar", lambda e: e.activation(
                    hbuf[:, oc, tt * 512:(tt + 1) * 512], C.ps[bank][:, :], AF.Gelu,
                    bias=b1g_sb[:, li, oc:oc + 1]))], enabled=PIPE["Dw1"])
            stream_loop(
                ser, 22, NTT, 8,
                lambda oc, par, li=li: (lambda e: e.dma_start(
                    out=wdb[:, par, :, :],
                    in_=dram["w1a"][li, :, oc * 128:(oc + 1) * 128].rearrange(
                        "(c p) f -> p c f", p=128))),
                lambda oc, tt, kc, par, bank: (lambda e: e.matmul(
                    C.ps[bank][:, :], wdb[:, par, kc, :],
                    xn[:, kc, tt * 512:(tt + 1) * 512],
                    start=(kc == 0), stop=(kc == 7))),
                lambda oc, tt, bank, li=li: [("vector", lambda e: e.scalar_tensor_tensor(
                    hbuf[:, oc, tt * 512:(tt + 1) * 512], C.ps[bank][:, :],
                    b1a_sb[:, li, oc:oc + 1],
                    hbuf[:, oc, tt * 512:(tt + 1) * 512], ALU.add, ALU.mult))],
                enabled=PIPE["Dw1"])
            stream_loop(
                ser, NCH, NTT, 22,
                lambda oc, par, li=li: (lambda e: e.dma_start(
                    out=wb22[:, par, :, :],
                    in_=dram["w2p"][li, :, oc * 128:(oc + 1) * 128].rearrange(
                        "(c p) f -> p c f", p=128))),
                lambda oc, tt, kc, par, bank: (lambda e: e.matmul(
                    C.ps[bank][:, :], wb22[:, par, kc, :],
                    hbuf[:, kc, tt * 512:(tt + 1) * 512],
                    start=(kc == 0), stop=(kc == 21))),
                lambda oc, tt, bank, li=li, t0=t0: [
                    ("scalar", lambda e: e.activation(
                        (C.sq if bank == 0 else dsq1)[:, 0:512],
                        C.ps[bank][:, :], AF.Identity,
                        bias=b2_sb[:, li, oc:oc + 1])),
                    ("vector", lambda e: e.tensor_tensor(
                        xs[:, oc, 1 + t0 + tt * 512:1 + t0 + (tt + 1) * 512],
                        (C.sq if bank == 0 else dsq1)[:, 0:512],
                        xs[:, oc, 1 + t0 + tt * 512:1 + t0 + (tt + 1) * 512],
                        ALU.add))], enabled=PIPE["Dw2"])
            if mt > 0:
                ser.v(lambda e, t0=t0: e.tensor_copy(xs[:, :, t0:t0 + 1],
                                                     curcol[:, :, :]))

    if final_ln:
        x_orig = dram["x_orig"]
        for mt in range(NMT):
            t0 = mt * MT
            ser.dmas(lambda e, t0=t0: e.dma_start(
                out=res[:, :, :],
                in_=x_orig.rearrange("(c p) t -> p c t", p=128)[:, :, t0:t0 + MT]))
            for oc in range(NCH):
                ser.v(lambda e, oc=oc, t0=t0: e.tensor_tensor(
                    res[:, oc, :], res[:, oc, :], xs[:, oc, 1 + t0:1 + t0 + MT],
                    ALU.add))
            emit_ln(ser, C, res, NCH, MT,
                    out_dma=lambda e, t0=t0: e.dma_start(
                        out=out_fm.rearrange("(c p) t -> p c t", p=128)[:, :, t0:t0 + MT],
                        in_=res[:, :, :]))
    else:
        ser.dmag(lambda e: e.dma_start(
            out=out_fm.rearrange("(c p) t -> p c t", p=128),
            in_=xs[:, :, 1:1 + T]))



from concourse.bass_utils import run_bass_kernel_spmd

NCORES = 8


def _host_prep(inputs):
    g = inputs
    h = {}
    h["w_in_t"] = np.ascontiguousarray(g["L0_attn_W_in"].T)
    h["w_out_t"] = np.ascontiguousarray(g["L0_attn_W_out"].T)
    WxT = np.ascontiguousarray(g["L0_attn_W_x"].T)
    w_x_aug = np.zeros((2048, 65), np.float32)
    w_x_aug[:, 0:16] = WxT[:, 1:17]
    w_x_aug[:, 32:48] = WxT[:, 17:33]
    w_x_aug[:, 64] = WxT[:, 0]
    h["w_x_aug"] = w_x_aug
    h["convw_s"] = np.ascontiguousarray(
        g["L0_attn_conv_w"][:, 0, :].reshape(16, 128, 4).transpose(1, 0, 2))
    h["convb_s"] = np.ascontiguousarray(g["L0_attn_conv_b"].reshape(16, 128).T)
    h["d_s"] = np.ascontiguousarray(g["L0_attn_D"].reshape(16, 128).T)
    h["alog_s"] = np.ascontiguousarray(g["L0_attn_A_log"].reshape(1, 16))
    h["ones_1p"] = np.ones((1, 128), np.float32)
    h["ones16"] = np.ones((16, 1), np.float32)
    h["ones128"] = np.ones((128, 1), np.float32)
    h["w1t"] = np.ascontiguousarray(g["L0_ffn_W1"].T)
    h["w2t"] = np.ascontiguousarray(g["L0_ffn_W2"].T)
    h["b1_s"] = np.ascontiguousarray(g["L0_ffn_b1"].reshape(32, 128).T)
    h["b2_s"] = np.ascontiguousarray(g["L0_ffn_b2"].reshape(8, 128).T)
    h["w_write_t"] = np.ascontiguousarray(g["L1_attn_W_write"].T)
    h["w_read_t"] = np.ascontiguousarray(g["L1_attn_W_read"].T)
    waug = np.zeros((1024, 72), np.float32)
    waug[:, 0:8] = g["L1_attn_W_alpha"].T
    waug[:, 64:72] = g["L1_attn_W_beta"].T
    h["waug"] = waug
    h["alog8"] = np.ascontiguousarray(g["L1_attn_A_log"].reshape(8, 1))
    h["dtb8"] = np.ascontiguousarray(g["L1_attn_dt_bias"].reshape(8, 1))
    h["trilneg"] = (-np.tril(np.ones((128, 128)), -1)).astype(np.float32)
    h["triu1"] = np.triu(np.ones((128, 128)), 1).astype(np.float32)
    h["ident"] = np.eye(128, dtype=np.float32)
    h["ones2d"] = np.ones((128, 128), np.float32)
    DEPTH = 6
    W1 = g["L1_ffn_W1"]
    W2 = g["L1_ffn_W2"]
    b1 = g["L1_ffn_b1"]
    w1a = np.zeros((DEPTH, 1024, 2816), np.float32)
    w1g = np.zeros((DEPTH, 1024, 2816), np.float32)
    w1a[:, :, 0:2730] = W1[:, :, 0:2730]
    w1g[:, :, 0:2730] = W1[:, :, 2730:5460]
    w2p = np.zeros((DEPTH, 2816, 1024), np.float32)
    w2p[:, 0:2730] = W2
    b1a = np.zeros((DEPTH, 2816), np.float32)
    b1g = np.zeros((DEPTH, 2816), np.float32)
    b1a[:, 0:2730] = b1[:, 0:2730]
    b1g[:, 0:2730] = b1[:, 2730:5460]
    tr = lambda a, nch: np.ascontiguousarray(
        a.reshape(DEPTH, nch, 128).transpose(2, 0, 1))
    h["wqkv"] = g["L1_ffn_Wqkv"]
    h["wo"] = g["L1_ffn_Wo"]
    h["w1a"], h["w1g"], h["w2p"] = w1a, w1g, w2p
    h["b1a_s"], h["b1g_s"] = tr(b1a, 22), tr(b1g, 22)
    h["b2_s2"] = tr(g["L1_ffn_b2"], 8)
    h["lng_s"] = tr(g["L1_ffn_ln_g"], 8)
    h["lnb_s"] = tr(g["L1_ffn_ln_b"], 8)
    h["qs_s"] = np.ascontiguousarray(
        g["L1_ffn_q_scale"].reshape(DEPTH, 1, 64).transpose(1, 0, 2))
    h["ks_s"] = np.ascontiguousarray(
        g["L1_ffn_k_scale"].reshape(DEPTH, 1, 64).transpose(1, 0, 2))
    DH, Wn = 64, 256
    half = np.arange(0, DH, 2, dtype=np.float64)
    invf = 1.0 / 10000.0 ** (half / DH)
    t = np.arange(T, dtype=np.float64)
    freqs = np.concatenate([t[:, None] * invf[None, :]] * 2, -1)
    bs = (half + 0.4 * DH) / (1.4 * DH)
    power = (t - T // 2) / (Wn // 2)
    sc = np.concatenate([bs[None, :] ** power[:, None]] * 2, -1)
    cos, sin = np.cos(freqs), np.sin(freqs)
    rep = lambda a: np.ascontiguousarray(np.tile(a.T, (2, 1)).astype(np.float32))
    h["csq8"] = rep(cos * sc * 8.0)
    h["ssq8"] = rep(sin * sc * 8.0)
    h["csk"] = rep(cos / sc)
    h["ssk"] = rep(sin / sc)
    prot = np.zeros((128, 128), np.float32)
    for blk in range(2):
        for f in range(32):
            prot[blk * 64 + f, blk * 64 + f + 32] = -1.0
            prot[blk * 64 + f + 32, blk * 64 + f] = 1.0
    h["protT"] = np.ascontiguousarray(prot.T)
    NEG = -60000.0
    mstd = np.full((128, 384), NEG, np.float32)
    for r in range(128):
        mstd[r, r:r + 257] = 0.0
    mq0 = np.full((128, 128), NEG, np.float32)
    for r in range(128):
        mq0[r, 0:r + 1] = 0.0
    mq1 = np.full((128, 256), NEG, np.float32)
    for r in range(128):
        mq1[r, 0:r + 129] = 0.0
    h["mstd"], h["mq0"], h["mq1"] = mstd, mq0, mq1
    h["mstdT"] = np.ascontiguousarray(np.concatenate(
        [mstd[:, 256:384].T, mstd[:, 128:256].T, mstd[:, 0:128].T], axis=1))
    h["mq0T"] = np.ascontiguousarray(mq0.T)
    h["mq1T"] = np.ascontiguousarray(np.concatenate(
        [mq1[:, 0:128].T, mq1[:, 128:256].T], axis=1))
    return h


def _declare(nc, host, names, renames=None):
    renames = renames or {}
    dram = {}
    for k in names:
        v = host[k]
        dram[renames.get(k, k)] = nc.dram_tensor(k, list(v.shape), F32,
                                                 kind="ExternalInput")
    return dram


def _run(nc, host, names, x_list, renames=None):
    in_maps = [dict({k: host[k] for k in names}, x_fm=x_list[i])
               for i in range(NCORES)]
    res = run_bass_kernel_spmd(nc, in_maps, core_ids=list(range(NCORES)))
    return [np.asarray(r["out_fm"], np.float32) for r in res.results]


SEM_BASE = 190
from contextlib import ExitStack


def _emit_prog(nc, ser, launch_idx):
    finals = dict.fromkeys(Ser.STREAMS, 0)
    for (en, pos, inc, fn, after) in ser.ops:
        finals[en] += inc
    with ExitStack() as st:
        semgrid = {}
        num = SEM_BASE
        for y in Ser.STREAMS:
            n = finals[y] // Ser.SEG + 1
            semgrid[y] = [st.enter_context(nc.semaphore(f"c_{y}{j}", num + j))
                          for j in range(n)]
            num += n
        block = st.enter_context(nc.Block())
        ser.emit(block, semgrid)


def kernel(**inputs):
    inputs = {k: np.asarray(v, np.float32) for k, v in inputs.items()}
    host = _host_prep(inputs)
    x0 = inputs["L0_attn_x"]
    x_list = [np.ascontiguousarray(x0[i].T) for i in range(NCORES)]

    names = ["w_in_t", "w_out_t", "w_x_aug", "convw_s", "convb_s", "d_s",
             "alog_s", "ones_1p", "ones16", "ones128",
             "w1t", "w2t", "b1_s", "b2_s",
             "w_write_t", "w_read_t", "waug", "alog8", "dtb8", "trilneg",
             "triu1", "ident", "ones2d",
             "wqkv", "wo", "w1a", "w1g", "w2p", "b1a_s", "b1g_s", "b2_s2",
             "lng_s", "lnb_s", "qs_s", "ks_s", "csq8", "ssq8", "csk", "ssk",
             "protT", "mstd", "mq0", "mq1", "mstdT", "mq0T", "mq1T"]
    nc = bass.Bass("TRN2", target_bir_lowering=False)
    dram = _declare(nc, host, names, renames={"b2_s2": "b2_s_d"})
    x_in = nc.dram_tensor("x_fm", [DM, T], F32, kind="ExternalInput")
    out = nc.dram_tensor("out_fm", [DM, T], F32, kind="ExternalOutput")
    xi1 = nc.dram_tensor("xi1", [DM, T], F32, kind="Internal")
    xi2 = nc.dram_tensor("xi2", [DM, T], F32, kind="Internal")
    xi3 = nc.dram_tensor("xi3", [DM, T], F32, kind="Internal")
    rows_scratch = nc.dram_tensor("rows_scratch", [2, 8, 1024], F32, kind="Internal")
    stat_scratch = nc.dram_tensor("stat_scratch", [128, 128], F32, kind="Internal")

    ser = Ser(nc)
    C = Common(nc)
    C.load(ser, dram)
    mark = nc.sbuf_base

    dA_ = dict(dram, x_fm=x_in, out_fm=xi1)
    build_l0_attn(ser, nc, C, dA_)
    nc.sbuf_base = mark

    dB_ = dict(dram, x_fm=xi1, out_fm=xi2)
    build_l0_ffn(ser, nc, C, dB_)
    nc.sbuf_base = mark

    dC_ = dict(dram, x_fm=xi2, out_fm=xi3, rows_scratch=rows_scratch)
    build_l1_attn(ser, nc, C, dC_)
    nc.sbuf_base = mark

    dD_ = dict(dram, x_fm=xi3, out_fm=out, x_orig=xi3,
               stat_scratch=stat_scratch, b2_s=dram["b2_s_d"])
    build_l1_ffn(ser, nc, C, dD_, layers=[0, 1, 2, 3, 4, 5], final_ln=True)

    _emit_prog(nc, ser, 0)

    in_maps = [dict({k: host[k] for k in names}, x_fm=x_list[i])
               for i in range(NCORES)]
    res = run_bass_kernel_spmd(nc, in_maps, core_ids=list(range(NCORES)))
    x_list = [np.asarray(r["out_fm"], np.float32) for r in res.results]
    outa = np.stack([x_list[i].T for i in range(NCORES)], axis=0)
    return np.ascontiguousarray(outa.astype(np.float32))


def kernel_unfused(**inputs):
    inputs = {k: np.asarray(v, np.float32) for k, v in inputs.items()}
    host = _host_prep(inputs)
    x0 = inputs["L0_attn_x"]
    x_list = [np.ascontiguousarray(x0[i].T) for i in range(NCORES)]

    namesA = ["w_in_t", "w_out_t", "w_x_aug", "convw_s", "convb_s", "d_s",
              "alog_s", "ones_1p", "ones16", "ones128"]
    nc = bass.Bass("TRN2", target_bir_lowering=False)
    dram = _declare(nc, host, namesA)
    dram["x_fm"] = nc.dram_tensor("x_fm", [DM, T], F32, kind="ExternalInput")
    dram["out_fm"] = nc.dram_tensor("out_fm", [DM, T], F32, kind="ExternalOutput")
    ser = Ser(nc)
    build_l0_attn(ser, nc, dram)
    _emit_prog(nc, ser, 0)
    x_list = _run(nc, host, namesA, x_list)

    namesB = ["w1t", "w2t", "b1_s", "b2_s", "ones_1p", "ones128"]
    nc = bass.Bass("TRN2", target_bir_lowering=False)
    dram = _declare(nc, host, namesB)
    dram["x_fm"] = nc.dram_tensor("x_fm", [DM, T], F32, kind="ExternalInput")
    dram["out_fm"] = nc.dram_tensor("out_fm", [DM, T], F32, kind="ExternalOutput")
    ser = Ser(nc)
    C = Common(nc)
    C.load(ser, dram)
    build_l0_ffn(ser, nc, C, dram)
    _emit_prog(nc, ser, 1)
    x_list = _run(nc, host, namesB, x_list)

    namesC = ["w_write_t", "w_read_t", "waug", "alog8", "dtb8", "trilneg",
              "triu1", "ident", "ones2d", "ones_1p", "ones128"]
    nc = bass.Bass("TRN2", target_bir_lowering=False)
    dram = _declare(nc, host, namesC)
    dram["x_fm"] = nc.dram_tensor("x_fm", [DM, T], F32, kind="ExternalInput")
    dram["out_fm"] = nc.dram_tensor("out_fm", [DM, T], F32, kind="ExternalOutput")
    dram["rows_scratch"] = nc.dram_tensor("rows_scratch", [2, 8, 1024], F32,
                                          kind="Internal")
    ser = Ser(nc)
    C = Common(nc)
    C.load(ser, dram)
    build_l1_attn(ser, nc, C, dram)
    _emit_prog(nc, ser, 2)
    x_list = _run(nc, host, namesC, x_list)

    namesD = ["wqkv", "wo", "w1a", "w1g", "w2p", "b1a_s", "b1g_s", "b2_s2",
              "lng_s", "lnb_s", "qs_s", "ks_s", "csq8", "ssq8", "csk", "ssk",
              "protT", "mstd", "mq0", "mq1", "mstdT", "mq0T", "mq1T",
              "ident", "ones_1p", "ones128"]
    groups = [[0, 1, 2], [3, 4, 5]]
    x3_list = x_list
    for gi, layers in enumerate(groups):
        last = gi == len(groups) - 1
        nc = bass.Bass("TRN2", target_bir_lowering=False)
        dram = _declare(nc, host, namesD, renames={"b2_s2": "b2_s"})
        dram["x_fm"] = nc.dram_tensor("x_fm", [DM, T], F32, kind="ExternalInput")
        dram["out_fm"] = nc.dram_tensor("out_fm", [DM, T], F32,
                                        kind="ExternalOutput")
        dram["stat_scratch"] = nc.dram_tensor("stat_scratch", [128, 128], F32,
                                              kind="Internal")
        if last:
            dram["x_orig"] = nc.dram_tensor("x_orig", [DM, T], F32,
                                            kind="ExternalInput")
        ser = Ser(nc)
        C = Common(nc)
        C.load(ser, dram)
        build_l1_ffn(ser, nc, C, dram, layers=layers, final_ln=last)
        _emit_prog(nc, ser, 3 + gi)
        if last:
            in_maps = [dict({k: host[k] for k in namesD}, x_fm=x_list[i],
                            x_orig=x3_list[i]) for i in range(NCORES)]
            res_ = run_bass_kernel_spmd(nc, in_maps, core_ids=list(range(NCORES)))
            x_list = [np.asarray(r["out_fm"], np.float32) for r in res_.results]
        else:
            x_list = _run(nc, host, namesD, x_list)

    out = np.stack([x_list[i].T for i in range(NCORES)], axis=0)
    return np.ascontiguousarray(out.astype(np.float32))


# version 31 (final)
# speedup vs baseline: 1.2319x; 1.2319x over previous
import numpy as np
import concourse.bass as bass
import concourse.mybir as mybir

F32 = mybir.dt.float32
BF16 = mybir.dt.bfloat16
AF = mybir.ActivationFunctionType
ALU = mybir.AluOpType
AX = mybir.AxisListType

T = 4096
DM = 1024
NCH = DM // 128


class Ser:

    def __init__(self, nc):
        self.nc = nc
        self.ops = []
        self.total = 0

    def add(self, eng, fn, after=None, inc=1):
        self.ops.append((eng, self.total, inc, fn, after))
        self.total += inc

    def dma(self, eng, fn, after=None):
        self.ops.append((eng, self.total, 16, fn, after))
        self.total += 16

    def v(self, fn, after=None, inc=1): self.add("vector", fn, after, inc)
    def s(self, fn, after=None, inc=1): self.add("scalar", fn, after, inc)
    def t(self, fn, after=None, inc=1): self.add("tensor", fn, after, inc)
    def g(self, fn, after=None, inc=1): self.add("gpsimd", fn, after, inc)
    def dmas(self, fn, after=None): self.dma("sync", fn, after)
    def dmag(self, fn, after=None): self.dma("gpsimd", fn, after)

    SEG = 48000
    STREAMS = ("vector", "scalar", "tensor", "gpsimd", "sync")

    def emit(self, block, semgrid, clear_sems=None):
        SEG = Ser.SEG
        S = Ser.STREAMS
        pre = []
        snap = {0: dict.fromkeys(S, 0)}
        cur = dict.fromkeys(S, 0)
        tot = 0
        for (en, pos, inc, fn, after) in self.ops:
            pre.append(dict(cur))
            cur[en] += inc
            tot += inc
            snap[tot] = dict(cur)
        finals = dict(cur)
        for y in S:
            need = finals[y] // SEG + 1
            assert need <= len(semgrid[y]), (y, finals[y], len(semgrid[y]))

        def wait_upto(eng, covered, y, target):
            if target <= covered[y]:
                return
            sems = semgrid[y]
            j = (target - 1) // SEG
            cj = (covered[y] - 1) // SEG if covered[y] > 0 else 0
            for jj in range(cj, j):
                full = min(finals[y] - jj * SEG, SEG)
                eng.wait_ge(sems[jj], full)
            eng.wait_ge(sems[j], target - j * SEG)
            covered[y] = target

        aug = []
        prev = None
        for idx, (en, pos, inc, fn, after) in enumerate(self.ops):
            aug.append((en, idx, inc, fn, after, prev))
            prev = en
        for name in S:
            ops = [o for o in aug if o[0] == name]
            if not ops and name != "sync":
                continue

            def body(eng, ops=ops, name=name):
                covered = dict.fromkeys(S, 0)
                own = 0
                is_dma_q = name in ("gpsimd", "sync")
                for (_, idx, inc, fn, after, pe_) in ops:
                    dep = pre[idx] if after is None else snap[after]
                    skip = (after is None and name == "tensor"
                            and pe_ == "tensor")
                    if not skip:
                        for y in S:
                            if y == name and name == "tensor":
                                continue
                            wait_upto(eng, covered, y, dep[y])
                    if is_dma_q:
                        wait_upto(eng, covered, name, own)
                    ins = fn(eng)
                    if inc:
                        j = own // SEG
                        ins.then_inc(semgrid[name][j], inc)
                        own += inc
                if name == "sync":
                    for y in S:
                        wait_upto(eng, covered, y, finals[y])
                    for cs in (clear_sems or sum(semgrid.values(), [])):
                        eng.sem_clear(cs)

            getattr(block, name)(body)


class Common:

    def __init__(self, nc, MT=1024):
        self.MT = MT
        al = nc.alloc_sbuf_tensor
        self.ones_1p = al("C_ones1p", [1, 128], F32)
        self.ones128 = al("C_ones128", [128, 1], F32)
        self.r0 = al("C_r0", [1, MT + 256], F32)
        self.r1 = al("C_r1", [1, MT + 256], F32)
        self.r2 = al("C_r2", [1, MT], F32)
        self.sq = al("C_sq", [128, 512], F32)
        ap = nc.alloc_psum_tensor
        self.ps = [ap(f"C_ps{i}", [128, 512], F32) for i in range(7)]
        self.psbf = ap("C_psbf", [128, 256], BF16)

    def load(self, ser, dram):
        ser.dmas(lambda e: e.dma_start(out=self.ones_1p[:, :], in_=dram["ones_1p"][:, :]))
        ser.dmas(lambda e: e.dma_start(out=self.ones128[:, :], in_=dram["ones128"][:, :]))


def emit_ln(ser, C, res, nch, MT, out_dma=None):
    psr0, psr1 = C.ps[3], C.ps[4]
    psb0, psb1 = C.ps[5], C.ps[6]
    NTT = MT // 512
    DMf = nch * 128
    for tt in range(NTT):
        sl = slice(tt * 512, (tt + 1) * 512)
        for oc in range(nch):
            ser.t(lambda e, oc=oc, sl=sl: e.matmul(
                psr0[0:1, :], C.ones128[:, :], res[:, oc, sl],
                start=(oc == 0), stop=(oc == nch - 1)))
        for oc in range(nch):
            ser.s(lambda e, oc=oc, sl=sl: e.activation(
                C.sq[:, 0:512], res[:, oc, sl], AF.Square))
            ser.t(lambda e, oc=oc, sl=sl: e.matmul(
                psr1[0:1, :], C.ones128[:, :], C.sq[:, 0:512],
                start=(oc == 0), stop=(oc == nch - 1)))
        ser.s(lambda e, sl=sl: e.copy(C.r0[:, sl], psr0[0:1, :]))
        ser.s(lambda e, sl=sl: e.copy(C.r1[:, sl], psr1[0:1, :]))
    ser.v(lambda e: e.tensor_scalar_mul(C.r0[:, 0:MT], C.r0[:, 0:MT], 1.0 / DMf))
    ser.v(lambda e: e.tensor_scalar(C.r1[:, 0:MT], C.r1[:, 0:MT], 1.0 / DMf, 1e-5,
                                    ALU.mult, ALU.add))
    ser.v(lambda e: e.tensor_tensor(C.r2[:, 0:MT], C.r0[:, 0:MT], C.r0[:, 0:MT],
                                    ALU.mult))
    ser.v(lambda e: e.tensor_tensor(C.r1[:, 0:MT], C.r1[:, 0:MT], C.r2[:, 0:MT],
                                    ALU.subtract))
    ser.s(lambda e: e.activation(C.r1[:, 0:MT], C.r1[:, 0:MT], AF.Ln))
    ser.s(lambda e: e.activation(C.r1[:, 0:MT], C.r1[:, 0:MT], AF.Exp, scale=-0.5))
    ser.v(lambda e: e.scalar_tensor_tensor(C.r2[:, 0:MT], C.r0[:, 0:MT], -1.0,
                                           C.r1[:, 0:MT], ALU.mult, ALU.mult))
    for tt in range(NTT):
        sl = slice(tt * 512, (tt + 1) * 512)
        ser.t(lambda e, sl=sl: e.matmul(psb0[:, :], C.ones_1p[:, :], C.r1[:, sl],
                                        start=True, stop=True))
        ser.t(lambda e, sl=sl: e.matmul(psb1[:, :], C.ones_1p[:, :], C.r2[:, sl],
                                        start=True, stop=True))
        for oc in range(nch):
            ser.v(lambda e, oc=oc, sl=sl: e.tensor_tensor(
                res[:, oc, sl], res[:, oc, sl], psb0[:, :], ALU.mult))
            ser.v(lambda e, oc=oc, sl=sl: e.tensor_tensor(
                res[:, oc, sl], res[:, oc, sl], psb1[:, :], ALU.add))
    if out_dma is not None:
        ser.dmas(out_dma)

def ceil_div(a, b):
    return (a + b - 1) // b


PIPE = {"A": True, "B": True, "Dq": True, "Dk": True, "Dwo": True, "Dw1": True, "Dw2": True}


def stream_loop(ser, nocs, ntiles, nkc, dma_fn, mm_fn, evac_fn, enabled=True):
    mmg_last = {}
    evac_by_bank = {}
    dma_ov = enabled in (True, "dma")
    ev_ov = enabled in (True, "evac")
    for oc in range(nocs):
        par = oc % 2
        ser.dma("gpsimd", dma_fn(oc, par),
                after=(mmg_last.get(oc - 2) if dma_ov else None))
        dma_end = ser.total
        for tt in range(ntiles):
            bank = tt % 2
            deps = [dma_end]
            if bank in evac_by_bank:
                deps.append(evac_by_bank[bank])
            for kc in range(nkc):
                ser.add("tensor", mm_fn(oc, tt, kc, par, bank),
                        after=(max(deps) if (kc == 0 and (dma_ov or ev_ov))
                               else None),
                        inc=(1 if kc == nkc - 1 else 0))
            mmg_last[oc] = ser.total
            first = True
            for eng, fn in evac_fn(oc, tt, bank):
                ser.add(eng, fn,
                        after=(mmg_last[oc] if (first and ev_ov) else None))
                first = False
            evac_by_bank[bank] = ser.total



def build_l0_attn(ser, nc, C, dram, dbg=None):
    dbg = dbg or {}
    def dump(name, buf):
        if name in dbg:
            h = dbg[name]
            ser.dmas(lambda e, h=h, buf=buf: e.dma_start(out=h[:, :], in_=buf))
    """dram: dict of DRAM tensor handles:
      x_fm (1024,4096) f32 in ; out_fm (1024,4096) f32 out
      w_in_t (1024,4096) ; w_out_t (2048,1024) ; w_x_aug (2048,65)
      convw_s (128,16,4) ; convb_s (128,16) ; d_s (128,16) ; alog_s (1,16)
      ones_1p (1,128) f32 ; ones16 (16,1) f32 ; ones128 (128,1) f32
    """
    MT = 1024
    NMT = T // MT
    NTT = MT // 512

    al = nc.alloc_sbuf_tensor
    xbf = al("A_xbf", [128, NCH, MT], BF16)
    wbuf = al("A_wbuf", [128, 2, 8, 128], BF16)
    wxbuf = al("A_wxbuf", [128, 16, 65], BF16)
    wobuf = al("A_wobuf", [128, 2, 16, 128], BF16)
    convw = al("A_convw", [128, 16, 4], F32)
    convb = al("A_convb", [128, 16], F32)
    dsb = al("A_dsb", [128, 16], F32)
    negA = al("A_negA", [1, 16], F32)
    ones_1p = C.ones_1p
    ones16 = al("A_ones16", [16, 1], F32)
    ones128 = C.ones128
    xc = al("A_xc", [128, 16, 3 + MT], BF16)
    carry = al("A_carry", [128, 16, 3], BF16)
    z = al("A_z", [128, 16, MT], BF16)
    convacc = al("A_convacc", [128, MT], F32)
    cbuf = al("A_cbuf", [16, MT], F32)
    delta = al("A_delta", [1, MT], F32)
    sprow = al("A_sprow", [1, MT], F32)
    dA = al("A_dA", [16, MT], F32)
    dBx = al("A_dBx", [16, MT], F32)
    hseq = dBx
    hcol = al("A_hcol", [16, 1], F32)
    yrow = al("A_yrow", [1, MT], F32)
    zsil = al("A_zsil", [128, 512], BF16)
    res = al("A_res", [128, NCH, MT], F32)
    srow = delta
    sqrow = sprow
    murow = yrow
    rstd = dA[0:1, :]
    nmur = dBx[0:1, :]

    ps0 = C.ps[0]
    ps1 = C.ps[1]
    psx = C.ps[2][0:65, :]
    ps16 = C.ps[3][0:16, :]
    psr0 = C.ps[4][0:1, :]
    psr1 = C.ps[4][32:33, :]
    psb0 = C.ps[5]
    psb1 = C.ps[6]

    x_fm, out_fm = dram["x_fm"], dram["out_fm"]
    w_in_t, w_out_t, w_x_aug = dram["w_in_t"], dram["w_out_t"], dram["w_x_aug"]

    ser.dmag(lambda e: e.dma_start(out=wxbuf[:, :, :],
                                   in_=w_x_aug.rearrange("(c p) f -> p c f", p=128)))
    ser.dmas(lambda e: e.dma_start(out=convw[:, :, :], in_=dram["convw_s"][:, :, :]))
    ser.dmas(lambda e: e.dma_start(out=convb[:, :], in_=dram["convb_s"][:, :]))
    ser.dmas(lambda e: e.dma_start(out=dsb[:, :], in_=dram["d_s"][:, :]))
    ser.dmas(lambda e: e.dma_start(out=negA[:, :], in_=dram["alog_s"][:, :]))
    ser.dmas(lambda e: e.dma_start(out=ones16[:, :], in_=dram["ones16"][:, :]))
    ser.s(lambda e: e.activation(negA[:, :], negA[:, :], AF.Exp))
    ser.v(lambda e: e.tensor_scalar_mul(negA[:, :], negA[:, :], -1.0))
    ser.v(lambda e: e.memset(xc[:, :, 0:3], 0.0))

    for mt in range(NMT):
        t0 = mt * MT
        ser.dmag(lambda e, t0=t0: e.dma_start(
            out=xbf[:, :, :],
            in_=x_fm.rearrange("(c p) t -> p c t", p=128)[:, :, t0:t0 + MT]))

        def _xz_dst(oc, tt):
            if oc < 16:
                return xc[:, oc, 3 + tt * 512: 3 + (tt + 1) * 512]
            return z[:, oc - 16, tt * 512:(tt + 1) * 512]
        stream_loop(
            ser, 32, NTT, 8,
            lambda oc, par: (lambda e: e.dma_start(
                out=wbuf[:, par, :, :],
                in_=w_in_t[:, oc * 128:(oc + 1) * 128].rearrange(
                    "(c p) f -> p c f", p=128))),
            lambda oc, tt, kc, par, bank: (lambda e: e.matmul(
                C.ps[bank][:, :], wbuf[:, par, kc, :],
                xbf[:, kc, tt * 512:(tt + 1) * 512],
                start=(kc == 0), stop=(kc == 7))),
            lambda oc, tt, bank: [("scalar", lambda e: e.copy(
                _xz_dst(oc, tt), C.ps[bank][:, :]))], enabled=PIPE["A"])

        if mt == 0:
            dump("xzraw", xc[:, 0, 3:3 + MT])
            dump("zraw", z[:, 0, 0:MT])
        if mt < NMT - 1:
            ser.v(lambda e: e.tensor_copy(carry[:, :, :], xc[:, :, MT:MT + 3]))
        for c in range(16):
            ser.v(lambda e, c=c: e.tensor_scalar(
                convacc[:, :], xc[:, c, 0:MT], convw[:, c, 0:1], None, ALU.mult))
            for k in range(1, 4):
                ser.v(lambda e, c=c, k=k: e.scalar_tensor_tensor(
                    convacc[:, :], xc[:, c, k:k + MT],
                    convw[:, c, k:k + 1], convacc[:, :], ALU.mult, ALU.add))
            ser.s(lambda e, c=c: e.activation(
                xc[:, c, 3:3 + MT], convacc[:, :], AF.Silu,
                bias=convb[:, c:c + 1]))
        if mt < NMT - 1:
            ser.v(lambda e: e.tensor_copy(xc[:, :, 0:3], carry[:, :, :]))

        for tt in range(NTT):
            for kc in range(16):
                ser.t(lambda e, tt=tt, kc=kc: e.matmul(
                    psx[:, :], wxbuf[:, kc, :],
                    xc[:, kc, 3 + tt * 512:3 + (tt + 1) * 512],
                    start=(kc == 0), stop=(kc == 15)))
            sl = slice(tt * 512, (tt + 1) * 512)
            ser.s(lambda e, sl=sl: e.copy(dBx[:, sl], psx[0:16, :]))
            ser.s(lambda e, sl=sl: e.copy(cbuf[:, sl], psx[32:48, :]))
            ser.s(lambda e, sl=sl: e.copy(yrow[:, sl], psx[64:65, :]))

        if mt == 0:
            dump("xcpost", xc[:, 0, 3:3 + MT])
            dump("bm", dBx[:, 0:MT])
            dump("cc", cbuf[:, 0:MT])
            dump("draw", yrow[:, 0:MT])
        ser.s(lambda e: e.activation(sprow[:, :], yrow[:, :], AF.Abs))
        ser.s(lambda e: e.activation(sprow[:, :], sprow[:, :], AF.Exp, scale=-1.0))
        ser.s(lambda e: e.activation(sprow[:, :], sprow[:, :], AF.Ln, bias=1.0))
        ser.v(lambda e: e.tensor_scalar_max(delta[:, :], yrow[:, :], 0.0))
        ser.v(lambda e: e.tensor_tensor(delta[:, :], delta[:, :], sprow[:, :], ALU.add))
        for tt in range(NTT):
            sl = slice(tt * 512, (tt + 1) * 512)
            ser.t(lambda e, sl=sl: e.matmul(ps16[:, :], negA[:, :], delta[:, sl],
                                            start=True, stop=True))
            ser.s(lambda e, sl=sl: e.activation(dA[:, sl], ps16[:, :], AF.Exp))
            ser.t(lambda e, sl=sl: e.matmul(ps16[:, :], ones_1p[:, 0:16], delta[:, sl],
                                            start=True, stop=True))
            ser.v(lambda e, sl=sl: e.tensor_tensor(dBx[:, sl], dBx[:, sl],
                                                   ps16[:, :], ALU.mult))
            ser.v(lambda e, sl=sl: e.tensor_tensor(
                dBx[:, sl], dBx[:, sl], xc[0:16, 0, 3 + sl.start:3 + sl.start + 512],
                ALU.mult))
        if mt == 0:
            dump("delta", delta[:, 0:MT])
            dump("dA", dA[:, 0:MT])
            dump("dBx", dBx[:, 0:MT])
            ser.v(lambda e: e.memset(hcol[:, :], 0.0))
        ser.v(lambda e: e.tensor_tensor_scan(hseq[:, :], dA[:, :], dBx[:, :],
                                             hcol[:, :], ALU.mult, ALU.add))
        if mt == 0:
            dump("hseq", hseq[:, 0:MT])
        ser.v(lambda e: e.tensor_copy(hcol[:, :], hseq[:, MT - 1:MT]))
        ser.v(lambda e: e.tensor_tensor(hseq[:, :], hseq[:, :], cbuf[:, :], ALU.mult))
        for tt in range(NTT):
            sl = slice(tt * 512, (tt + 1) * 512)
            ser.t(lambda e, sl=sl: e.matmul(psr0[:, :], ones16[:, :], hseq[:, sl],
                                            start=True, stop=True))
            ser.s(lambda e, sl=sl: e.copy(yrow[:, sl], psr0[:, :]))

        if mt == 0:
            dump("yrow", yrow[:, 0:MT])
        for c in range(16):
            for tt in range(NTT):
                sl = slice(tt * 512, (tt + 1) * 512)
                ser.t(lambda e, sl=sl: e.matmul(psb0[:, :], ones_1p[:, :], yrow[:, sl],
                                                start=True, stop=True))
                ser.s(lambda e, c=c, sl=sl: e.activation(zsil[:, :], z[:, c, sl], AF.Silu))
                ser.v(lambda e, c=c, sl=sl: e.scalar_tensor_tensor(
                    convacc[:, sl], xc[:, c, 3 + sl.start:3 + sl.start + 512],
                    dsb[:, c:c + 1], psb0[:, :], ALU.mult, ALU.add))
                ser.v(lambda e, c=c, sl=sl: e.tensor_tensor(
                    z[:, c, sl], convacc[:, sl], zsil[:, :], ALU.mult))

        if mt == 0:
            dump("yf0", z[:, 0, 0:MT])
        stream_loop(
            ser, NCH, NTT, 16,
            lambda oc, par: (lambda e: e.dma_start(
                out=wobuf[:, par, :, :],
                in_=w_out_t[:, oc * 128:(oc + 1) * 128].rearrange(
                    "(c p) f -> p c f", p=128))),
            lambda oc, tt, kc, par, bank: (lambda e: e.matmul(
                C.ps[bank][:, :], wobuf[:, par, kc, :],
                z[:, kc, tt * 512:(tt + 1) * 512],
                start=(kc == 0), stop=(kc == 15))),
            lambda oc, tt, bank: [("vector", lambda e: e.tensor_tensor(
                res[:, oc, tt * 512:(tt + 1) * 512], C.ps[bank][:, :],
                xbf[:, oc, tt * 512:(tt + 1) * 512], ALU.add))],
            enabled=PIPE["A"])

        if mt == 0:
            dump("res0", res[:, 0, 0:MT])
        for tt in range(NTT):
            sl = slice(tt * 512, (tt + 1) * 512)
            for oc in range(NCH):
                ser.t(lambda e, oc=oc, sl=sl: e.matmul(
                    psr0[:, :], ones128[:, :], res[:, oc, sl],
                    start=(oc == 0), stop=(oc == NCH - 1)))
            for oc in range(NCH):
                ser.s(lambda e, oc=oc, sl=sl: e.activation(
                    convacc[:, sl], res[:, oc, sl], AF.Square))
                ser.t(lambda e, oc=oc, sl=sl: e.matmul(
                    psr1[:, :], ones128[:, :], convacc[:, sl],
                    start=(oc == 0), stop=(oc == NCH - 1)))
            ser.s(lambda e, sl=sl: e.copy(srow[:, sl], psr0[:, :]))
            ser.s(lambda e, sl=sl: e.copy(sqrow[:, sl], psr1[:, :]))
        ser.v(lambda e: e.tensor_scalar_mul(murow[:, :], srow[:, :], 1.0 / DM))
        ser.v(lambda e: e.tensor_scalar(sqrow[:, :], sqrow[:, :], 1.0 / DM, 1e-5,
                                        ALU.mult, ALU.add))
        ser.v(lambda e: e.tensor_tensor(rstd, murow[:, :], murow[:, :], ALU.mult))
        ser.v(lambda e: e.tensor_tensor(rstd, sqrow[:, :], rstd, ALU.subtract))
        ser.s(lambda e: e.activation(rstd, rstd, AF.Ln))
        ser.s(lambda e: e.activation(rstd, rstd, AF.Exp, scale=-0.5))
        ser.v(lambda e: e.scalar_tensor_tensor(nmur, murow[:, :], -1.0,
                                               rstd, ALU.mult, ALU.mult))
        for tt in range(NTT):
            sl = slice(tt * 512, (tt + 1) * 512)
            ser.t(lambda e, sl=sl: e.matmul(psb0[:, :], ones_1p[:, :],
                                            dA[0:1, sl], start=True, stop=True))
            ser.t(lambda e, sl=sl: e.matmul(psb1[:, :], ones_1p[:, :],
                                            dBx[0:1, sl], start=True, stop=True))
            for oc in range(NCH):
                ser.v(lambda e, oc=oc, sl=sl: e.tensor_tensor(
                    res[:, oc, sl], res[:, oc, sl], psb0[:, :], ALU.mult))
                ser.v(lambda e, oc=oc, sl=sl: e.tensor_tensor(
                    res[:, oc, sl], res[:, oc, sl], psb1[:, :], ALU.add))
        ser.dmas(lambda e, t0=t0: e.dma_start(
            out=out_fm.rearrange("(c p) t -> p c t", p=128)[:, :, t0:t0 + MT],
            in_=res[:, :, :]))



def build_l0_ffn(ser, nc, C, dram, dbg=None):
    dbg = dbg or {}
    def dump(name, buf):
        if name in dbg:
            h = dbg[name]
            ser.dmas(lambda e, h=h, buf=buf: e.dma_start(out=h[:, :], in_=buf))

    MT = 1024
    NMT = T // MT
    NTT = MT // 512

    al = nc.alloc_sbuf_tensor
    xbf = al("B_xbf", [128, NCH, MT], BF16)
    wf = al("B_wf", [128, 2, 32, 128], F32)
    wt = al("B_wt", [128, 2, 32, 128], BF16)
    wt2 = al("B_wt2", [128, 2, 32, 128], BF16)
    h = al("B_h", [128, 32, MT], BF16)
    res = al("B_res", [128, NCH, MT], F32)
    b1sb = al("B_b1", [128, 32], F32)
    b2sb = al("B_b2", [128, NCH], F32)
    colsums = al("B_colsums", [128, 32], F32)
    Bsq2 = al("B_sq2", [128, 2, 512], F32)
    thr = al("B_thr", [128, 2, 2], F32)
    scal = al("B_scal", [1, 2], F32)

    x_fm, out_fm = dram["x_fm"], dram["out_fm"]
    w1t, w2t = dram["w1t"], dram["w2t"]

    ser.dmas(lambda e: e.dma_start(out=b1sb[:, :], in_=dram["b1_s"][:, :]))
    ser.dmas(lambda e: e.dma_start(out=b2sb[:, :], in_=dram["b2_s"][:, :]))

    for wi, (wd, ncol) in enumerate([(w1t, 4096), (w2t, 1024)]):
        nsl = ncol // 128
        for oc in range(nsl):
            wsl = wd[:, oc * 128:(oc + 1) * 128]
            nkc = wd.shape[0] // 128
            ser.dmas(lambda e, wsl=wsl, nkc=nkc: e.dma_start(
                out=wf[:, 0, 0:nkc, :], in_=wsl.rearrange("(c p) f -> p c f", p=128)))
            ser.v(lambda e, oc=oc, nkc=nkc: e.reduce_sum(
                colsums[:, oc:oc + 1], wf[:, 0, 0:nkc, :], AX.XY,
                apply_absolute_value=True))
        ser.v(lambda e, nsl=nsl: e.reduce_sum(colsums[:, 0:1], colsums[:, 0:nsl], AX.X))
        ser.t(lambda e: e.matmul(C.ps[0][0:1, 0:1], C.ones128[:, :], colsums[:, 0:1],
                                 start=True, stop=True))
        ser.s(lambda e, wi=wi: e.copy(scal[:, wi:wi + 1], C.ps[0][0:1, 0:1]))
        ser.v(lambda e, wi=wi: e.tensor_scalar_mul(scal[:, wi:wi + 1], scal[:, wi:wi + 1],
                                                   1.0 / (4096.0 * 1024.0)))
        ser.v(lambda e, wi=wi: e.tensor_scalar_max(scal[:, wi:wi + 1], scal[:, wi:wi + 1],
                                                   1e-8))
        ser.v(lambda e, wi=wi: e.tensor_scalar_mul(scal[:, wi:wi + 1], scal[:, wi:wi + 1],
                                                   0.5))
        ser.t(lambda e, wi=wi: e.matmul(C.ps[0][:, 0:1], C.ones_1p[:, :],
                                        scal[:, wi:wi + 1], start=True, stop=True))
        ser.s(lambda e, wi=wi: e.copy(thr[:, wi, 0:1], C.ps[0][:, 0:1]))
        ser.v(lambda e, wi=wi: e.tensor_scalar_mul(thr[:, wi, 1:2], thr[:, wi, 0:1], -1.0))

    def ternarize(nkc, wi, par):
        ser.v(lambda e, nkc=nkc, wi=wi, par=par: e.tensor_scalar(
            wt[:, par, 0:nkc, :], wf[:, par, 0:nkc, :], thr[:, wi, 0:1], None,
            ALU.is_gt))
        ser.v(lambda e, nkc=nkc, wi=wi, par=par: e.tensor_scalar(
            wt2[:, par, 0:nkc, :], wf[:, par, 0:nkc, :], thr[:, wi, 1:2], None,
            ALU.is_lt))
        ser.v(lambda e, nkc=nkc, par=par: e.tensor_tensor(
            wt[:, par, 0:nkc, :], wt[:, par, 0:nkc, :], wt2[:, par, 0:nkc, :],
            ALU.subtract))

    for mt in range(NMT):
        t0 = mt * MT
        ser.dmag(lambda e, t0=t0: e.dma_start(
            out=xbf[:, :, :],
            in_=x_fm.rearrange("(c p) t -> p c t", p=128)[:, :, t0:t0 + MT]))
        mm_last = {}
        evac_by_bank = {}
        for oc in range(32):
            par = oc % 2
            wsl = w1t[:, oc * 128:(oc + 1) * 128]
            ser.dmas(lambda e, wsl=wsl, par=par: e.dma_start(
                out=wf[:, par, 0:8, :],
                in_=wsl.rearrange("(c p) f -> p c f", p=128)),
                after=(mm_last.get(oc - 2) if PIPE["B"] else None))
            ternarize(8, 0, par)
            prep_end = ser.total
            for tt in range(NTT):
                bank = tt % 2
                deps = [prep_end]
                if bank in evac_by_bank:
                    deps.append(evac_by_bank[bank])
                for kc in range(8):
                    ser.t(lambda e, bank=bank, kc=kc, tt=tt, par=par: e.matmul(
                        C.ps[bank][:, :], wt[:, par, kc, :],
                        xbf[:, kc, tt * 512:(tt + 1) * 512],
                        start=(kc == 0), stop=(kc == 7)),
                        after=(max(deps) if (kc == 0 and PIPE["B"]) else None),
                        inc=(1 if kc == 7 else 0))
                mm_last[oc] = ser.total
                ser.s(lambda e, oc=oc, tt=tt, bank=bank: e.activation(
                    h[:, oc, tt * 512:(tt + 1) * 512], C.ps[bank][:, :], AF.Gelu,
                    bias=b1sb[:, oc:oc + 1]),
                    after=(mm_last[oc] if PIPE["B"] else None))
                evac_by_bank[bank] = ser.total
        if mt == 0:
            dump("h0", h[:, 0, 0:MT])
        mm_last = {}
        evac_by_bank = {}
        for oc in range(NCH):
            par = oc % 2
            wsl = w2t[:, oc * 128:(oc + 1) * 128]
            ser.dmas(lambda e, wsl=wsl, par=par: e.dma_start(
                out=wf[:, par, 0:32, :],
                in_=wsl.rearrange("(c p) f -> p c f", p=128)),
                after=(mm_last.get(oc - 2) if PIPE["B"] else None))
            ternarize(32, 1, par)
            prep_end = ser.total
            for tt in range(NTT):
                bank = tt % 2
                deps = [prep_end]
                if bank in evac_by_bank:
                    deps.append(evac_by_bank[bank])
                for kc in range(32):
                    ser.t(lambda e, bank=bank, kc=kc, tt=tt, par=par: e.matmul(
                        C.ps[bank][:, :], wt[:, par, kc, :],
                        h[:, kc, tt * 512:(tt + 1) * 512],
                        start=(kc == 0), stop=(kc == 31)),
                        after=(max(deps) if (kc == 0 and PIPE["B"]) else None),
                        inc=(1 if kc == 31 else 0))
                mm_last[oc] = ser.total
                ser.s(lambda e, oc=oc, tt=tt, bank=bank: e.activation(
                    Bsq2[:, bank, :], C.ps[bank][:, :], AF.Identity,
                    bias=b2sb[:, oc:oc + 1]),
                    after=(mm_last[oc] if PIPE["B"] else None))
                ser.v(lambda e, oc=oc, tt=tt, bank=bank: e.tensor_tensor(
                    res[:, oc, tt * 512:(tt + 1) * 512],
                    Bsq2[:, bank, :],
                    xbf[:, oc, tt * 512:(tt + 1) * 512], ALU.add))
                evac_by_bank[bank] = ser.total
        if mt == 0:
            dump("res0", res[:, 0, 0:MT])
        emit_ln(ser, C, res, NCH, MT,
                out_dma=lambda e, t0=t0: e.dma_start(
                    out=out_fm.rearrange("(c p) t -> p c t", p=128)[:, :, t0:t0 + MT],
                    in_=res[:, :, :]))



def build_l1_attn(ser, nc, C, dram, dbg=None):
    dbg = dbg or {}
    def dump(name, buf, shape2=None):
        if name in dbg:
            h = dbg[name]
            ser.dmas(lambda e, h=h, buf=buf: e.dma_start(out=h[:, :], in_=buf))

    MT = 1024
    NMT = T // MT
    NTT = MT // 512
    NCK = MT // 128
    H = 8

    al = nc.alloc_sbuf_tensor
    xbf = al("D_xbf", [128, NCH, MT], BF16)
    wwr = al("D_wwr", [128, 8, 1024], BF16)
    wrd = al("D_wrd", [128, 8, 1024], BF16)
    waugb = al("D_waug", [128, 8, 72], BF16)
    rk = al("D_rk", [128, H, 1 + MT], BF16)
    rkcar = al("D_rkcar", [128, H, 1], BF16)
    vsc = al("D_vsc", [128, NCK, 1024], BF16)
    ofm = al("D_ofm", [128, H, MT], BF16)
    res = al("D_res", [128, NCH, MT], F32)
    cumr = al("D_cumr", [8, MT], F32)
    lnbr = al("D_lnbr", [8, MT], F32)
    cendbr = al("D_cendbr", [8, MT], F32)
    betar = al("D_betar", [8, MT], F32)
    tailr = al("D_tailr", [8, MT], F32)
    er = al("D_er", [8, MT], F32)
    m1r = al("D_m1r", [8, MT], F32)
    colbuf = al("D_colbuf", [128, NCK, 40], F32)
    rows2 = al("D_rows2", [1, 2, MT], BF16)
    ones1pb = al("D_ones1pb", [1, 128], BF16)
    a0 = al("D_a0", [8, MT], F32)
    sp = al("D_sp", [8, MT], F32)
    bp = al("D_bp", [8, MT], F32)
    zero8 = al("D_zero8", [8, 128], F32)
    negA8 = al("D_negA8", [8, 1], F32)
    dtb = al("D_dtb", [8, 1], F32)
    trilneg = al("D_trilneg", [128, 128], F32)
    triu1 = al("D_triu1", [128, 128], F32)
    ident = al("D_ident", [128, 128], F32)
    identbf = al("D_identbf", [128, 128], BF16)
    ones2d = al("D_ones2d", [128, 128], F32)
    DIFF = al("D_DIFF", [128, 128], F32)
    DIFFN = al("D_DIFFN", [128, 128], F32)
    e1 = al("D_e1", [128, 128], F32)
    Mt = al("D_M", [128, 128], F32)
    intraT = al("D_intraT", [128, 128], F32)
    Pb = al("D_P", [128, 128], F32)
    Qb = al("D_Q", [128, 128], F32)
    Bb = al("D_B", [128, 128], F32)
    wsdneg = al("D_wsdneg", [128, 128], F32)
    rkd = al("D_rkd", [128, 128], F32)
    wkt = al("D_wkt", [128, 128], F32)
    rhs0 = al("D_rhs0", [128, 128], F32)
    vnew = al("D_vnew", [128, 128], F32)
    osb = al("D_osb", [128, 128], BF16)
    cedc = al("D_cedc", [128, 1], F32)
    S = al("D_S", [128, H, 128], F32)

    psT = C.psbf

    x_fm, out_fm = dram["x_fm"], dram["out_fm"]
    rowsd = dram["rows_scratch"]

    ser.dmag(lambda e: e.dma_start(
        out=wwr[:, :, :], in_=dram["w_write_t"].rearrange("(c p) f -> p c f", p=128)))
    ser.dmag(lambda e: e.dma_start(
        out=wrd[:, :, :], in_=dram["w_read_t"].rearrange("(c p) f -> p c f", p=128)))
    ser.dmag(lambda e: e.dma_start(
        out=waugb[:, :, :], in_=dram["waug"].rearrange("(c p) f -> p c f", p=128)))
    ser.dmag(lambda e: e.dma_start(out=identbf[:, :], in_=dram["ident"][:, :]))
    ser.dmas(lambda e: e.dma_start(out=negA8[:, :], in_=dram["alog8"][:, :]))
    ser.dmas(lambda e: e.dma_start(out=dtb[:, :], in_=dram["dtb8"][:, :]))
    ser.dmas(lambda e: e.dma_start(out=trilneg[:, :], in_=dram["trilneg"][:, :]))
    ser.dmas(lambda e: e.dma_start(out=triu1[:, :], in_=dram["triu1"][:, :]))
    ser.dmas(lambda e: e.dma_start(out=ident[:, :], in_=dram["ident"][:, :]))
    ser.dmas(lambda e: e.dma_start(out=ones2d[:, :], in_=dram["ones2d"][:, :]))
    ser.s(lambda e: e.activation(negA8[:, :], negA8[:, :], AF.Exp))
    ser.v(lambda e: e.tensor_scalar_mul(negA8[:, :], negA8[:, :], -1.0))
    ser.v(lambda e: e.memset(zero8[:, :], 0.0))
    ser.dmag(lambda e: e.dma_start(out=ones1pb[:, :], in_=dram["ones_1p"][:, :]))
    ser.v(lambda e: e.memset(rk[:, :, 0:1], 0.0))
    ser.v(lambda e: e.memset(S[:, :, :], 0.0))

    for mt in range(NMT):
        t0 = mt * MT
        ser.dmag(lambda e, t0=t0: e.dma_start(
            out=xbf[:, :, :],
            in_=x_fm.rearrange("(c p) t -> p c t", p=128)[:, :, t0:t0 + MT]))

        for h in range(H):
            for tt in range(NTT):
                sl = slice(tt * 512, (tt + 1) * 512)
                ser.s(lambda e, h=h, sl=sl: e.activation(
                    C.sq[:, 0:512], xbf[:, h, sl], AF.Square))
                ser.t(lambda e, sl=sl: e.matmul(
                    C.ps[3][0:1, :], C.ones128[:, :], C.sq[:, 0:512],
                    start=True, stop=True))
                ser.s(lambda e, sl=sl: e.copy(C.r0[:, sl], C.ps[3][0:1, :]))
            ser.v(lambda e: e.tensor_scalar_max(C.r0[:, 0:MT], C.r0[:, 0:MT], 1e-24))
            ser.s(lambda e: e.activation(C.r0[:, 0:MT], C.r0[:, 0:MT], AF.Ln))
            ser.s(lambda e: e.activation(C.r0[:, 0:MT], C.r0[:, 0:MT], AF.Exp,
                                         scale=-0.5))
            for tt in range(NTT):
                sl = slice(tt * 512, (tt + 1) * 512)
                ser.t(lambda e, sl=sl: e.matmul(
                    C.ps[5][:, :], C.ones_1p[:, :], C.r0[:, sl],
                    start=True, stop=True))
                ser.v(lambda e, h=h, sl=sl: e.tensor_tensor(
                    rk[:, h, 1 + sl.start:1 + sl.start + 512], xbf[:, h, sl],
                    C.ps[5][:, :], ALU.mult))

        for tt in range(NTT):
            sl = slice(tt * 512, (tt + 1) * 512)
            for kc in range(8):
                ser.t(lambda e, kc=kc, sl=sl: e.matmul(
                    C.ps[0][0:72, :], waugb[:, kc, :], xbf[:, kc, sl],
                    start=(kc == 0), stop=(kc == 7)))
            ser.s(lambda e, sl=sl: e.activation(
                a0[:, sl], C.ps[0][0:8, :], AF.Identity, bias=dtb[:, 0:1]))
            ser.s(lambda e, sl=sl: e.copy(bp[:, sl], C.ps[0][64:72, :]))
        ser.s(lambda e: e.activation(sp[:, :], a0[:, :], AF.Abs))
        ser.s(lambda e: e.activation(sp[:, :], sp[:, :], AF.Exp, scale=-1.0))
        ser.s(lambda e: e.activation(sp[:, :], sp[:, :], AF.Ln, bias=1.0))
        ser.v(lambda e: e.tensor_scalar_max(a0[:, :], a0[:, :], 0.0))
        ser.v(lambda e: e.tensor_tensor(a0[:, :], a0[:, :], sp[:, :], ALU.add))
        ser.v(lambda e: e.tensor_scalar(a0[:, :], a0[:, :], negA8[:, 0:1], None,
                                        ALU.mult))
        for ck in range(NCK):
            csl = slice(ck * 128, (ck + 1) * 128)
            ser.v(lambda e, csl=csl: e.tensor_tensor_scan(
                cumr[:, csl], a0[:, csl], zero8[:, :], 0.0, ALU.add, ALU.add))
        ser.s(lambda e: e.activation(sp[:, :], bp[:, :], AF.Abs))
        ser.s(lambda e: e.activation(sp[:, :], sp[:, :], AF.Exp, scale=-1.0))
        ser.s(lambda e: e.activation(sp[:, :], sp[:, :], AF.Ln, bias=1.0))
        ser.v(lambda e: e.tensor_scalar_max(a0[:, :], bp[:, :], 0.0))
        ser.v(lambda e: e.tensor_tensor(a0[:, :], bp[:, :], a0[:, :], ALU.subtract))
        ser.v(lambda e: e.tensor_tensor(lnbr[:, :], a0[:, :], sp[:, :],
                                        ALU.subtract))
        ser.s(lambda e: e.activation(betar[:, :], bp[:, :], AF.Sigmoid))
        for ck in range(NCK):
            csl = slice(ck * 128, (ck + 1) * 128)
            ser.v(lambda e, ck=ck, csl=csl: e.tensor_scalar(
                cendbr[:, csl], zero8[:, :],
                cumr[0:8, ck * 128 + 127:ck * 128 + 128], 1.0,
                ALU.add, ALU.mult))
        ser.v(lambda e: e.tensor_tensor(tailr[:, :], cendbr[:, :], cumr[:, :],
                                        ALU.subtract))
        ser.s(lambda e: e.activation(tailr[:, :], tailr[:, :], AF.Exp))
        ser.s(lambda e: e.activation(er[:, :], cumr[:, :], AF.Exp))
        ser.v(lambda e: e.scalar_tensor_tensor(m1r[:, :], betar[:, :], -1.0,
                                               er[:, :], ALU.mult, ALU.mult))
        ser.dmas(lambda e: e.dma_start(out=rowsd[0, :, :], in_=m1r[:, :]))
        ser.dmas(lambda e: e.dma_start(out=rowsd[1, :, :], in_=er[:, :]))
        for qi, qt in enumerate([cumr, lnbr, cendbr, betar, tailr]):
            for ck in range(NCK):
                csl = slice(ck * 128, (ck + 1) * 128)
                ser.t(lambda e, qt=qt, csl=csl: e.transpose(
                    C.ps[2][0:128, 0:8], qt[:, csl], ident[0:8, 0:8]))
                ser.s(lambda e, ck=ck, qi=qi: e.copy(
                    colbuf[:, ck, qi * 8:(qi + 1) * 8], C.ps[2][0:128, 0:8]))
        ser.s(lambda e: e.activation(colbuf[:, :, 16:24], colbuf[:, :, 16:24], AF.Exp))

        for tk in range(NCK):
            for vb in range(2):
                psd = C.ps[vb]
                for kc in range(8):
                    ser.t(lambda e, psd=psd, kc=kc, tk=tk, vb=vb: e.matmul(
                        psd[:, :], xbf[:, kc, tk * 128:(tk + 1) * 128],
                        wwr[:, kc, vb * 512:(vb + 1) * 512],
                        start=(kc == 0), stop=(kc == 7)))
                for hh in range(4):
                    h = vb * 4 + hh
                    ser.v(lambda e, psd=psd, tk=tk, vb=vb, hh=hh, h=h: e.tensor_scalar(
                        vsc[:, tk, h * 128:(h + 1) * 128],
                        psd[:, hh * 128:(hh + 1) * 128],
                        colbuf[:, tk, 24 + h:25 + h], None, ALU.mult))

        for h in range(H):
            ser.dmag(lambda e, h=h: e.dma_start(out=rows2[:, 0, :], in_=rowsd[0, h:h + 1, :]))
            ser.dmag(lambda e, h=h: e.dma_start(out=rows2[:, 1, :], in_=rowsd[1, h:h + 1, :]))
            for ck in range(NCK):
                rks = rk[:, h, 1 + ck * 128:1 + ck * 128 + 128]
                wks = rk[:, h, ck * 128:ck * 128 + 128]
                ccol = colbuf[:, ck, h:h + 1]
                lnbcol = colbuf[:, ck, 8 + h:9 + h]
                cedcol = colbuf[:, ck, 16 + h:17 + h]
                tailcol = colbuf[:, ck, 32 + h:33 + h]
                m1row = rows2[:, 0, ck * 128:(ck + 1) * 128]
                erow = rows2[:, 1, ck * 128:(ck + 1) * 128]
                Ssl = S[:, h, :]

                ser.v(lambda e, ccol=ccol: e.tensor_scalar(
                    DIFF[:, :], ones2d[:, :], ccol, None, ALU.mult))
                ser.t(lambda e: e.transpose(C.ps[2][:, 0:128], DIFF[:, :], ident[:, :]))
                ser.v(lambda e: e.scalar_tensor_tensor(
                    DIFF[:, :], C.ps[2][:, 0:128], -1.0, DIFF[:, :],
                    ALU.mult, ALU.add))
                ser.v(lambda e: e.tensor_scalar(
                    DIFFN[:, :], DIFF[:, :], -1.0, 0.0, ALU.mult, ALU.min))
                ser.v(lambda e: e.tensor_scalar_min(DIFF[:, :], DIFF[:, :], 0.0))
                ser.s(lambda e, lnbcol=lnbcol: e.activation(
                    e1[:, :], DIFF[:, :], AF.Exp, bias=lnbcol))
                ser.s(lambda e: e.activation(DIFFN[:, :], DIFFN[:, :], AF.Exp))
                ser.t(lambda e, wks=wks: e.matmul(C.ps[0][:, 0:128], wks, wks,
                                                  start=True, stop=True))
                ser.v(lambda e: e.tensor_tensor(Mt[:, :], C.ps[0][:, 0:128],
                                                e1[:, :], ALU.mult))
                ser.v(lambda e: e.tensor_tensor(Mt[:, :], Mt[:, :], trilneg[:, :],
                                                ALU.mult))
                ser.t(lambda e, wks=wks, rks=rks: e.matmul(
                    C.ps[1][:, 0:128], wks, rks, start=True, stop=True))
                ser.v(lambda e: e.tensor_tensor(intraT[:, :], C.ps[1][:, 0:128],
                                                DIFFN[:, :], ALU.mult))
                ser.v(lambda e: e.tensor_tensor(intraT[:, :], intraT[:, :],
                                                triu1[:, :], ALU.mult))
                ser.t(lambda e: e.transpose(C.ps[2][:, 0:128], Mt[:, :], ident[:, :]))
                ser.v(lambda e: e.tensor_copy(Pb[:, :], C.ps[2][:, 0:128]))
                ser.v(lambda e: e.tensor_tensor(Bb[:, :], C.ps[2][:, 0:128],
                                                ident[:, :], ALU.add))
                ser.v(lambda e: e.tensor_copy(Qb[:, :], Mt[:, :]))
                for lvl in range(6):
                    ser.t(lambda e: e.matmul(C.ps[1][:, 0:128], Pb[:, :], Qb[:, :],
                                             start=True, stop=True))
                    ser.t(lambda e: e.matmul(C.ps[2][:, 0:128], Qb[:, :], Pb[:, :],
                                             start=True, stop=True))
                    ser.v(lambda e: e.tensor_copy(Qb[:, :], C.ps[1][:, 0:128]))
                    ser.v(lambda e: e.tensor_copy(Pb[:, :], C.ps[2][:, 0:128]))
                    ser.t(lambda e: e.matmul(C.ps[0][:, 0:128], Qb[:, :], Bb[:, :],
                                             start=True, stop=True))
                    ser.v(lambda e: e.tensor_tensor(Bb[:, :], Bb[:, :],
                                                    C.ps[0][:, 0:128], ALU.add))
                ser.t(lambda e, m1row=m1row: e.matmul(
                    C.ps[0][:, 0:128], ones1pb[:, :], m1row, start=True, stop=True))
                ser.v(lambda e, wks=wks: e.tensor_tensor(
                    wsdneg[:, :], wks, C.ps[0][:, 0:128], ALU.mult))
                ser.t(lambda e, erow=erow: e.matmul(
                    C.ps[1][:, 0:128], ones1pb[:, :], erow, start=True, stop=True))
                ser.v(lambda e, rks=rks: e.tensor_tensor(
                    rkd[:, :], rks, C.ps[1][:, 0:128], ALU.mult))
                ser.t(lambda e, wks=wks: e.transpose(psT[:, 0:128], wks, identbf[:, :]))
                ser.v(lambda e, tailcol=tailcol: e.tensor_scalar(
                    wkt[:, :], psT[:, 0:128], tailcol, None, ALU.mult))
                ser.t(lambda e: e.matmul(C.ps[0][:, 0:128], wsdneg[:, :], Ssl,
                                         start=True, stop=True))
                ser.v(lambda e, ck=ck, h=h: e.tensor_tensor(
                    rhs0[:, :], C.ps[0][:, 0:128], vsc[:, ck, h * 128:(h + 1) * 128],
                    ALU.add))
                ser.t(lambda e: e.matmul(C.ps[1][:, 0:128], Bb[:, :], rhs0[:, :],
                                         start=True, stop=True))
                ser.v(lambda e: e.tensor_copy(vnew[:, :], C.ps[1][:, 0:128]))
                ser.t(lambda e: e.matmul(C.ps[0][:, 0:128], rkd[:, :], Ssl,
                                         start=True, stop=False))
                ser.t(lambda e: e.matmul(C.ps[0][:, 0:128], intraT[:, :], vnew[:, :],
                                         start=False, stop=True))
                ser.v(lambda e: e.tensor_copy(osb[:, :], C.ps[0][:, 0:128]))
                ser.t(lambda e: e.matmul(C.ps[1][:, 0:128], wkt[:, :], vnew[:, :],
                                         start=True, stop=True))
                ser.s(lambda e, cedcol=cedcol: e.copy(cedc[:, :], cedcol))
                ser.v(lambda e, Ssl=Ssl: e.scalar_tensor_tensor(
                    Ssl, Ssl, cedc[:, 0:1], C.ps[1][:, 0:128], ALU.mult, ALU.add))
                ser.t(lambda e: e.transpose(psT[:, 0:128], osb[:, :], identbf[:, :]))
                ser.s(lambda e, h=h, ck=ck: e.copy(
                    ofm[:, h, ck * 128:(ck + 1) * 128], psT[:, 0:128]))

        if mt < NMT - 1:
            ser.v(lambda e: e.tensor_copy(rkcar[:, :, :], rk[:, :, MT:MT + 1]))
            ser.v(lambda e: e.tensor_copy(rk[:, :, 0:1], rkcar[:, :, :]))

        for oc in range(NCH):
            for tt in range(NTT):
                sl = slice(tt * 512, (tt + 1) * 512)
                psd = C.ps[tt % 2]
                for kc in range(8):
                    ser.t(lambda e, psd=psd, kc=kc, oc=oc, sl=sl: e.matmul(
                        psd[:, :], wrd[:, kc, oc * 128:(oc + 1) * 128],
                        ofm[:, kc, sl], start=(kc == 0), stop=(kc == 7)))
                ser.v(lambda e, oc=oc, sl=sl, psd=psd: e.scalar_tensor_tensor(
                    res[:, oc, sl], xbf[:, oc, sl], 2.0, psd[:, :],
                    ALU.mult, ALU.add))
        if mt == 0:
            dump("res0", res[:, 0, 0:MT])
            dump("ofm0", ofm[:, 0, 0:MT])
        emit_ln(ser, C, res, NCH, MT,
                out_dma=lambda e, t0=t0: e.dma_start(
                    out=out_fm.rearrange("(c p) t -> p c t", p=128)[:, :, t0:t0 + MT],
                    in_=res[:, :, :]))



def build_l1_ffn(ser, nc, C, dram, layers, final_ln, dbg=None):
    dbg = dbg or {}
    def dump(name, buf):
        if name in dbg:
            h = dbg[name]
            ser.dmas(lambda e, h=h, buf=buf: e.dma_start(out=h[:, :], in_=buf))

    MT = 1024
    NMT = T // MT
    NTT = MT // 512
    NQT = MT // 128
    H16 = 16

    al = nc.alloc_sbuf_tensor
    xs = al("E_xs", [128, NCH, 1 + T], BF16)
    protT = al("E_protT", [128, 128], BF16)
    identbf = al("E_identbf", [128, 128], BF16)
    ones128b = al("E_ones128b", [128, 1], BF16)
    mstd = al("E_mstd", [128, 384], BF16)
    mq0 = al("E_mq0", [128, 128], BF16)
    mq1 = al("E_mq1", [128, 256], BF16)
    mstdT = al("E_mstdT", [128, 384], BF16)
    mq0T = al("E_mq0T", [128, 128], BF16)
    mq1T = al("E_mq1T", [128, 256], BF16)
    qs_sb = al("E_qs", [1, 6, 64], F32)
    ks_sb = al("E_ks", [1, 6, 64], F32)
    b1a_sb = al("E_b1a", [128, 6, 22], F32)
    b1g_sb = al("E_b1g", [128, 6, 22], F32)
    b2_sb = al("E_b2", [128, 6, 8], F32)
    lng_sb = al("E_lng", [128, 6, 8], F32)
    lnb_sb = al("E_lnb", [128, 6, 8], F32)
    wdb = al("E_wdb", [128, 2, 8, 128], BF16)
    wv = al("E_wv", [128, 8, 1024], BF16)
    qraw = al("E_qraw", [128, 8, MT], BF16)
    kraw = al("E_kraw", [128, 8, 1280], BF16)
    vtm = al("E_vtm", [128, 10, 1024], BF16)
    attno = al("E_attno", [128, 8, MT], BF16)
    qno = al("E_qno", [128, 512], BF16)
    t1b = al("E_t1b", [128, 512], BF16)
    halo = al("E_halo", [128, 8, 257], BF16)
    curcol = al("E_curcol", [128, 8, 1], BF16)
    aprev = al("E_aprev", [128, 8, 1], BF16)
    Pall = al("E_Pall", [128, 10, 384], BF16)
    scrap = nc.alloc_sbuf_tensor_at("E_scrap", [128, 384], BF16,
                                    offset=nc.lookup_mloc(Pall).addr)
    _ovl = nc.lookup_mloc(kraw).addr
    negmax = al("E_negmax", [128, 1], F32)
    sumcol = al("E_sumcol", [128, 1], F32)
    identf = al("E_identf", [128, 128], F32)
    dsq1 = al("E_dsq1", [128, 512], F32)
    rowbuf = al("E_rowbuf", [1, 8, 128], F32)
    hbuf = nc.alloc_sbuf_tensor_at("E_hbuf", [128, 22, MT], BF16, offset=_ovl)
    res = nc.alloc_sbuf_tensor_at("E_res", [128, NCH, MT], F32, offset=_ovl)
    wb22 = nc.alloc_sbuf_tensor_at("E_wb22", [128, 2, 22, 128], BF16,
                                   offset=_ovl + 22 * MT * 2)
    statc = nc.alloc_sbuf_tensor_at("E_statc", [128, 128], F32,
                                    offset=_ovl + 22 * MT * 2)
    tblq = nc.alloc_sbuf_tensor_at("E_tblq", [128, 2, 1024], BF16,
                                   offset=_ovl + 22 * MT * 2 + 512)
    tblk = nc.alloc_sbuf_tensor_at("E_tblk", [128, 2, 1280], BF16,
                                   offset=_ovl + 22 * MT * 2 + 512 + 4096)
    xn = qraw

    x_fm, out_fm = dram["x_fm"], dram["out_fm"]
    statd = dram["stat_scratch"]

    ser.v(lambda e: e.memset(xs[:, :, 0:1], 0.0))
    ser.dmag(lambda e: e.dma_start(
        out=xs[:, :, 1:1 + T], in_=x_fm.rearrange("(c p) t -> p c t", p=128)))
    for nm, buf in [("protT", protT), ("ident", identbf), ("mstd", mstd),
                    ("mq0", mq0), ("mq1", mq1), ("mstdT", mstdT),
                    ("mq0T", mq0T), ("mq1T", mq1T), ("ones128", ones128b)]:
        ser.dmag(lambda e, nm=nm, buf=buf: e.dma_start(out=buf[:, :], in_=dram[nm][:, :]))
    ser.dmas(lambda e: e.dma_start(out=identf[:, :], in_=dram["ident"][:, :]))
    for nm, buf in [("qs_s", qs_sb), ("ks_s", ks_sb), ("b1a_s", b1a_sb),
                    ("b1g_s", b1g_sb), ("b2_s", b2_sb), ("lng_s", lng_sb),
                    ("lnb_s", lnb_sb)]:
        ser.dmas(lambda e, nm=nm, buf=buf: e.dma_start(out=buf[:, :, :], in_=dram[nm][:, :, :]))

    def xs_sh(c, a, b):
        off = 1 if c < 4 else 0
        return xs[:, c, off + a: off + b]

    def ksrc_sh(c, mt, t0, lo, w):
        if mt == 0:
            return xs_sh(c, lo, lo + w)
        if lo < 256:
            assert lo + w <= 256
            off = 1 if c < 4 else 0
            return halo[:, c, off + lo: off + lo + w]
        return xs_sh(c, t0 + lo - 256, t0 + lo - 256 + w)

    for li in layers:
        wqkv = dram["wqkv"]
        for mt in range(NMT):
            t0 = mt * MT
            kstart = max(0, t0 - 256)
            klen = t0 + MT - kstart
            nkc = klen // 128
            ser.dmag(lambda e, t0=t0: e.dma_start(
                out=tblq[:, 0, 0:MT], in_=dram["csq8"][:, t0:t0 + MT]))
            ser.dmag(lambda e, t0=t0: e.dma_start(
                out=tblq[:, 1, 0:MT], in_=dram["ssq8"][:, t0:t0 + MT]))
            ser.dmag(lambda e, kstart=kstart, klen=klen: e.dma_start(
                out=tblk[:, 0, 0:klen], in_=dram["csk"][:, kstart:kstart + klen]))
            ser.dmag(lambda e, kstart=kstart, klen=klen: e.dma_start(
                out=tblk[:, 1, 0:klen], in_=dram["ssk"][:, kstart:kstart + klen]))

            if mt > 0:
                ser.v(lambda e, t0=t0: e.tensor_copy(curcol[:, :, :],
                                                     xs[:, :, t0:t0 + 1]))
                ser.v(lambda e, t0=t0: e.tensor_copy(xs[:, :, t0:t0 + 1],
                                                     halo[:, :, 256:257]))
            stream_loop(
                ser, 8, NTT, 8,
                lambda oc, par, li=li: (lambda e: e.dma_start(
                    out=wdb[:, par, :, :],
                    in_=wqkv[li, :, oc * 128:(oc + 1) * 128].rearrange(
                        "(c p) f -> p c f", p=128))),
                lambda oc, tt, kc, par, bank, t0=t0: (lambda e: e.matmul(
                    C.ps[bank][:, :], wdb[:, par, kc, :],
                    xs_sh(kc, t0 + tt * 512, t0 + (tt + 1) * 512),
                    start=(kc == 0), stop=(kc == 7))),
                lambda oc, tt, bank: [("scalar", lambda e: e.copy(
                    qraw[:, oc, tt * 512:(tt + 1) * 512], C.ps[bank][:, :]))],
                enabled=PIPE["Dq"])
            if mt == 0:
                kts = [(0, 512), (512, 512)]
            else:
                kts = [(0, 256), (256, 512), (768, 512)]
            stream_loop(
                ser, 8, len(kts), 8,
                lambda oc, par, li=li: (lambda e: e.dma_start(
                    out=wdb[:, par, :, :],
                    in_=wqkv[li, :, 1024 + oc * 128:1024 + (oc + 1) * 128].rearrange(
                        "(c p) f -> p c f", p=128))),
                lambda oc, tt, kc, par, bank, mt=mt, t0=t0, kts=kts: (
                    lambda e: e.matmul(
                        C.ps[bank][:, 0:kts[tt][1]], wdb[:, par, kc, :],
                        ksrc_sh(kc, mt, t0, kts[tt][0], kts[tt][1]),
                        start=(kc == 0), stop=(kc == 7))),
                lambda oc, tt, bank, kts=kts: [("scalar", lambda e: e.copy(
                    kraw[:, oc, kts[tt][0]:kts[tt][0] + kts[tt][1]],
                    C.ps[bank][:, 0:kts[tt][1]]))], enabled=PIPE["Dk"])
            ser.dmag(lambda e, li=li: e.dma_start(
                out=wv[:, :, :],
                in_=wqkv[li, :, 2048:3072].rearrange("(c p) f -> p c f", p=128)))
            for tk in range(nkc):
                for vb in range(2):
                    psd = C.ps[vb]
                    for kc in range(8):
                        ser.t(lambda e, psd=psd, kc=kc, tk=tk, vb=vb,
                              mt=mt, t0=t0: e.matmul(
                            psd[:, :],
                            ksrc_sh(kc, mt, t0, tk * 128, 128),
                            wv[:, kc, vb * 512:(vb + 1) * 512],
                            start=(kc == 0), stop=(kc == 7)))
                    ser.s(lambda e, tk=tk, vb=vb, psd=psd: e.copy(
                        vtm[:, tk, vb * 512:(vb + 1) * 512], psd[:, :]))
            if mt < NMT - 1:
                ser.v(lambda e, t0=t0: e.tensor_copy(
                    halo[:, :, :], xs[:, :, t0 + 768:t0 + 1025]))

            def norm_rope(buf, nlen, base, cs, ss, sc_sb, li):
                nt = [(s, min(512, nlen - s)) for s in range(0, nlen, 512)]
                for ch in range(8):
                    for (s, w) in nt:
                        ser.s(lambda e, ch=ch, s=s, w=w, buf=buf: e.activation(
                            C.sq[:, 0:w], buf[:, ch, s:s + w], AF.Square))
                        ser.t(lambda e, s=s, w=w: e.matmul(
                            C.ps[3][0:1, 0:w], C.ones128[0:64, :], C.sq[0:64, 0:w],
                            start=True, stop=True))
                        ser.t(lambda e, s=s, w=w: e.matmul(
                            C.ps[4][0:1, 0:w], C.ones128[64:128, :], C.sq[64:128, 0:w],
                            start=True, stop=True))
                        ser.s(lambda e, s=s, w=w: e.copy(C.r0[:, s:s + w], C.ps[3][0:1, 0:w]))
                        ser.s(lambda e, s=s, w=w: e.copy(C.r1[:, s:s + w], C.ps[4][0:1, 0:w]))
                    for rr in (C.r0, C.r1):
                        ser.v(lambda e, rr=rr, nlen=nlen: e.tensor_scalar_max(
                            rr[:, 0:nlen], rr[:, 0:nlen], 1e-24))
                        ser.s(lambda e, rr=rr, nlen=nlen: e.activation(
                            rr[:, 0:nlen], rr[:, 0:nlen], AF.Ln))
                        ser.s(lambda e, rr=rr, nlen=nlen: e.activation(
                            rr[:, 0:nlen], rr[:, 0:nlen], AF.Exp, scale=-0.5))
                    for (s, w) in nt:
                        ser.t(lambda e, s=s, w=w, li=li, sc_sb=sc_sb: e.matmul(
                            C.ps[5][0:64, 0:w], sc_sb[:, li, :], C.r0[:, s:s + w],
                            start=True, stop=True))
                        ser.t(lambda e, s=s, w=w, li=li, sc_sb=sc_sb: e.matmul(
                            C.ps[5][64:128, 0:w], sc_sb[:, li, :], C.r1[:, s:s + w],
                            start=True, stop=True, tile_position=(0, 64)))
                        ser.v(lambda e, ch=ch, s=s, w=w, buf=buf: e.tensor_tensor(
                            qno[:, 0:w], buf[:, ch, s:s + w], C.ps[5][:, 0:w],
                            ALU.mult))
                        ser.t(lambda e, s=s, w=w: e.matmul(
                            C.ps[6][:, 0:w], protT[:, :], qno[:, 0:w],
                            start=True, stop=True))
                        ser.v(lambda e, ch=ch, s=s, w=w, cs=cs: e.tensor_tensor(
                            t1b[:, 0:w], qno[:, 0:w], cs[:, s:s + w],
                            ALU.mult))
                        ser.v(lambda e, s=s, w=w, ss=ss: e.tensor_tensor(
                            qno[:, 0:w], C.ps[6][:, 0:w], ss[:, s:s + w],
                            ALU.mult))
                        ser.v(lambda e, ch=ch, s=s, w=w, buf=buf: e.tensor_tensor(
                            buf[:, ch, s:s + w], t1b[:, 0:w], qno[:, 0:w], ALU.add))
            norm_rope(qraw, MT, t0, tblq[:, 0], tblq[:, 1], qs_sb, li)
            norm_rope(kraw, klen, kstart, tblk[:, 0], tblk[:, 1], ks_sb, li)
            if mt == 0:
                dump("qpost", qraw[:, 0, 0:MT])
                dump("kpost", kraw[:, 0, 0:MT])
                dump("vt0", vtm[:, 0, 0:1024])

            for h in range(H16):
                pb = (h % 2) * 64
                chq = h // 2
                for qt in range(NQT):
                    if mt == 0 and qt == 0:
                        ks_, kw, msk = 0, 128, mq0
                    elif mt == 0 and qt == 1:
                        ks_, kw, msk = 0, 256, mq1
                    else:
                        ks_ = qt * 128 if mt > 0 else (qt - 2) * 128
                        kw, msk = 384, mstd
                    ser.t(lambda e, pb=pb, chq=chq, qt=qt, ks_=ks_, kw=kw: e.matmul(
                        C.ps[0][:, 0:kw],
                        qraw[pb:pb + 64, chq, qt * 128:(qt + 1) * 128],
                        kraw[pb:pb + 64, chq, ks_:ks_ + kw],
                        start=True, stop=False))
                    ser.t(lambda e, kw=kw, msk=msk: e.matmul(
                        C.ps[0][:, 0:kw], identbf[:, :], msk[:, 0:kw],
                        start=False, stop=True))
                    ser.v(lambda e, kw=kw: e.reduce_max(
                        negmax[:, :], C.ps[0][:, 0:kw], AX.X, negate=True))
                    ser.s(lambda e, kw=kw: e.activation(
                        scrap[:, 0:kw], C.ps[0][:, 0:kw], AF.Exp,
                        bias=negmax[:, 0:1], accum_out=sumcol[:, :]))
                    ser.s(lambda e: e.activation(sumcol[:, :], sumcol[:, :], AF.Ln))
                    ser.v(lambda e, h=h, qt=qt: e.tensor_tensor(
                        statc[:, qt * 16 + h:qt * 16 + h + 1], negmax[:, :],
                        sumcol[:, :], ALU.subtract))
            if mt == 0:
                dump("statc0", statc[:, :])
            if mt == 1:
                dump("statc1", statc[:, :])
            ser.t(lambda e: e.transpose(C.ps[2][:, 0:128], statc[:, :], identf[:, :]))
            ser.v(lambda e: e.tensor_copy(C.sq[:, 0:128], C.ps[2][:, 0:128]))
            ser.dmas(lambda e: e.dma_start(out=statd[:, :], in_=C.sq[:, 0:128]))

            for h in range(H16):
                pb = (h % 2) * 64
                chq = h // 2
                ser.dmas(lambda e, h=h: e.dma_start(
                    out=rowbuf[:, :, :],
                    in_=bass.AP(statd, h * 128, [[1, 1], [16 * 128, 8], [1, 128]])))
                spans = {}
                for kc in range(nkc):
                    qts = ([q for q in range(max(0, kc - 2), min(kc + 1, NQT))]
                           if mt > 0 else
                           [q for q in range(kc, min(kc + 3, NQT))])
                    spans[kc] = qts
                    if not qts:
                        continue
                    q0 = qts[0]
                    span = qts[-1] - q0 + 1
                    ser.t(lambda e, pb=pb, chq=chq, kc=kc, q0=q0, span=span: e.matmul(
                        C.ps[3][:, 0:span * 128],
                        kraw[pb:pb + 64, chq, kc * 128:(kc + 1) * 128],
                        qraw[pb:pb + 64, chq, q0 * 128:(q0 + span) * 128],
                        start=True, stop=False))
                    for q in qts:
                        if mt == 0 and q == 0:
                            mt_, mo = mq0T, 0
                        elif mt == 0 and q == 1:
                            mt_, mo = mq1T, kc * 128
                        else:
                            rel = (kc - q) if mt > 0 else (kc - q + 2)
                            mt_, mo = mstdT, (2 - rel) * 128
                        ser.t(lambda e, q=q, q0=q0, mt_=mt_, mo=mo: e.matmul(
                            C.ps[3][:, (q - q0) * 128:(q - q0 + 1) * 128],
                            identbf[:, :], mt_[:, mo:mo + 128],
                            start=False, stop=False))
                    ser.t(lambda e, q0=q0, span=span: e.matmul(
                        C.ps[3][:, 0:span * 128], C.ones_1p[:, :],
                        rowbuf[:, q0:q0 + span, :].rearrange("one a b -> one (a b)"),
                        start=False, stop=True))
                    ser.s(lambda e, span=span, kc=kc: e.activation(
                        Pall[:, kc, 0:span * 128], C.ps[3][:, 0:span * 128], AF.Exp))
                vsl0 = (h % 2) * 64 + (h // 2) * 128
                for q in range(NQT):
                    kcs = [kc for kc in range(nkc) if q in spans[kc]]
                    for j, kc in enumerate(kcs):
                        col = (q - spans[kc][0]) * 128
                        ser.t(lambda e, pb=pb, kc=kc, col=col, vsl0=vsl0,
                              st=(j == 0), sp=(j == len(kcs) - 1): e.matmul(
                            C.ps[0][pb:pb + 64, 0:128],
                            vtm[:, kc, vsl0:vsl0 + 64],
                            Pall[:, kc, col:col + 128],
                            start=st, stop=sp, tile_position=(0, pb)))
                    ser.s(lambda e, pb=pb, chq=chq, q=q: e.copy(
                        attno[pb:pb + 64, chq, q * 128:(q + 1) * 128],
                        C.ps[0][pb:pb + 64, 0:128]))

            if mt == 0:
                dump("attno0", attno[:, 0, 0:MT])
            if mt == 1:
                dump("attno1", attno[:, 0, 0:MT])
            stream_loop(
                ser, NCH, NTT, 8,
                lambda oc, par, li=li: (lambda e: e.dma_start(
                    out=wdb[:, par, :, :],
                    in_=dram["wo"][li, :, oc * 128:(oc + 1) * 128].rearrange(
                        "(c p) f -> p c f", p=128))),
                lambda oc, tt, kc, par, bank: (lambda e: e.matmul(
                    C.ps[bank][:, :], wdb[:, par, kc, :],
                    attno[:, kc, tt * 512:(tt + 1) * 512],
                    start=(kc == 0), stop=(kc == 7))),
                lambda oc, tt, bank, t0=t0: [("vector", lambda e: e.tensor_tensor(
                    xs[:, oc, 1 + t0 + tt * 512:1 + t0 + (tt + 1) * 512],
                    C.ps[bank][:, :],
                    xs[:, oc, 1 + t0 + tt * 512:1 + t0 + (tt + 1) * 512],
                    ALU.add))], enabled=PIPE["Dwo"])

            if mt == 0:
                dump("stream_a", xs[:, 0, 1:1 + MT])
            if mt > 0:
                ser.v(lambda e, t0=t0: e.tensor_copy(xs[:, :, t0:t0 + 1],
                                                     aprev[:, :, :]))
            if mt < NMT - 1:
                ser.v(lambda e, t0=t0: e.tensor_copy(
                    aprev[:, :, :], xs[:, :, t0 + MT:t0 + MT + 1]))
            for tt in range(NTT):
                sl = slice(tt * 512, (tt + 1) * 512)
                for oc in range(NCH):
                    ser.t(lambda e, oc=oc, tt=tt, t0=t0: e.matmul(
                        C.ps[3][0:1, :], ones128b[:, :],
                        xs_sh(oc, t0 + tt * 512, t0 + (tt + 1) * 512),
                        start=(oc == 0), stop=(oc == NCH - 1)))
                for oc in range(NCH):
                    ser.s(lambda e, oc=oc, tt=tt, t0=t0: e.activation(
                        C.sq[:, 0:512],
                        xs_sh(oc, t0 + tt * 512, t0 + (tt + 1) * 512), AF.Square))
                    ser.t(lambda e, oc=oc, tt=tt: e.matmul(
                        C.ps[4][0:1, :], C.ones128[:, :],
                        C.sq[:, 0:512],
                        start=(oc == 0), stop=(oc == NCH - 1)))
                ser.s(lambda e, sl=sl: e.copy(C.r0[:, sl], C.ps[3][0:1, :]))
                ser.s(lambda e, sl=sl: e.copy(C.r1[:, sl], C.ps[4][0:1, :]))
            ser.v(lambda e: e.tensor_scalar_mul(C.r0[:, 0:MT], C.r0[:, 0:MT],
                                                1.0 / DM))
            ser.v(lambda e: e.tensor_scalar(C.r1[:, 0:MT], C.r1[:, 0:MT],
                                            1.0 / DM, 1e-5, ALU.mult, ALU.add))
            ser.v(lambda e: e.tensor_tensor(C.r2[:, 0:MT], C.r0[:, 0:MT],
                                            C.r0[:, 0:MT], ALU.mult))
            ser.v(lambda e: e.tensor_tensor(C.r1[:, 0:MT], C.r1[:, 0:MT],
                                            C.r2[:, 0:MT], ALU.subtract))
            ser.s(lambda e: e.activation(C.r1[:, 0:MT], C.r1[:, 0:MT], AF.Ln))
            ser.s(lambda e: e.activation(C.r1[:, 0:MT], C.r1[:, 0:MT], AF.Exp,
                                         scale=-0.5))
            ser.v(lambda e: e.scalar_tensor_tensor(C.r2[:, 0:MT], C.r0[:, 0:MT],
                                                   -1.0, C.r1[:, 0:MT],
                                                   ALU.mult, ALU.mult))
            for tt in range(NTT):
                sl = slice(tt * 512, (tt + 1) * 512)
                ser.t(lambda e, sl=sl: e.matmul(C.ps[5][:, :], C.ones_1p[:, :],
                                                C.r1[:, sl], start=True, stop=True))
                ser.t(lambda e, sl=sl: e.matmul(C.ps[6][:, :], C.ones_1p[:, :],
                                                C.r2[:, sl], start=True, stop=True))
                for oc in range(NCH):
                    ser.v(lambda e, oc=oc, sl=sl, tt=tt, t0=t0: e.tensor_tensor(
                        xn[:, oc, sl],
                        xs_sh(oc, t0 + tt * 512, t0 + (tt + 1) * 512),
                        C.ps[5][:, :], ALU.mult))
                    ser.v(lambda e, oc=oc, sl=sl: e.tensor_tensor(
                        xn[:, oc, sl], xn[:, oc, sl], C.ps[6][:, :], ALU.add))
                    ser.v(lambda e, oc=oc, sl=sl, li=li: e.tensor_scalar(
                        xn[:, oc, sl], xn[:, oc, sl], lng_sb[:, li, oc:oc + 1],
                        lnb_sb[:, li, oc:oc + 1], ALU.mult, ALU.add))
            stream_loop(
                ser, 22, NTT, 8,
                lambda oc, par, li=li: (lambda e: e.dma_start(
                    out=wdb[:, par, :, :],
                    in_=dram["w1g"][li, :, oc * 128:(oc + 1) * 128].rearrange(
                        "(c p) f -> p c f", p=128))),
                lambda oc, tt, kc, par, bank: (lambda e: e.matmul(
                    C.ps[bank][:, :], wdb[:, par, kc, :],
                    xn[:, kc, tt * 512:(tt + 1) * 512],
                    start=(kc == 0), stop=(kc == 7))),
                lambda oc, tt, bank, li=li: [("scalar", lambda e: e.activation(
                    hbuf[:, oc, tt * 512:(tt + 1) * 512], C.ps[bank][:, :], AF.Gelu,
                    bias=b1g_sb[:, li, oc:oc + 1]))], enabled=PIPE["Dw1"])
            stream_loop(
                ser, 22, NTT, 8,
                lambda oc, par, li=li: (lambda e: e.dma_start(
                    out=wdb[:, par, :, :],
                    in_=dram["w1a"][li, :, oc * 128:(oc + 1) * 128].rearrange(
                        "(c p) f -> p c f", p=128))),
                lambda oc, tt, kc, par, bank: (lambda e: e.matmul(
                    C.ps[bank][:, :], wdb[:, par, kc, :],
                    xn[:, kc, tt * 512:(tt + 1) * 512],
                    start=(kc == 0), stop=(kc == 7))),
                lambda oc, tt, bank, li=li: [("vector", lambda e: e.scalar_tensor_tensor(
                    hbuf[:, oc, tt * 512:(tt + 1) * 512], C.ps[bank][:, :],
                    b1a_sb[:, li, oc:oc + 1],
                    hbuf[:, oc, tt * 512:(tt + 1) * 512], ALU.add, ALU.mult))],
                enabled=PIPE["Dw1"])
            stream_loop(
                ser, NCH, NTT, 22,
                lambda oc, par, li=li: (lambda e: e.dma_start(
                    out=wb22[:, par, :, :],
                    in_=dram["w2p"][li, :, oc * 128:(oc + 1) * 128].rearrange(
                        "(c p) f -> p c f", p=128))),
                lambda oc, tt, kc, par, bank: (lambda e: e.matmul(
                    C.ps[bank][:, :], wb22[:, par, kc, :],
                    hbuf[:, kc, tt * 512:(tt + 1) * 512],
                    start=(kc == 0), stop=(kc == 21))),
                lambda oc, tt, bank, li=li, t0=t0: [
                    ("scalar", lambda e: e.activation(
                        (C.sq if bank == 0 else dsq1)[:, 0:512],
                        C.ps[bank][:, :], AF.Identity,
                        bias=b2_sb[:, li, oc:oc + 1])),
                    ("vector", lambda e: e.tensor_tensor(
                        xs[:, oc, 1 + t0 + tt * 512:1 + t0 + (tt + 1) * 512],
                        (C.sq if bank == 0 else dsq1)[:, 0:512],
                        xs[:, oc, 1 + t0 + tt * 512:1 + t0 + (tt + 1) * 512],
                        ALU.add))], enabled=PIPE["Dw2"])
            if mt > 0:
                ser.v(lambda e, t0=t0: e.tensor_copy(xs[:, :, t0:t0 + 1],
                                                     curcol[:, :, :]))

    if final_ln:
        x_orig = dram["x_orig"]
        for mt in range(NMT):
            t0 = mt * MT
            ser.dmas(lambda e, t0=t0: e.dma_start(
                out=res[:, :, :],
                in_=x_orig.rearrange("(c p) t -> p c t", p=128)[:, :, t0:t0 + MT]))
            for oc in range(NCH):
                ser.v(lambda e, oc=oc, t0=t0: e.tensor_tensor(
                    res[:, oc, :], res[:, oc, :], xs[:, oc, 1 + t0:1 + t0 + MT],
                    ALU.add))
            emit_ln(ser, C, res, NCH, MT,
                    out_dma=lambda e, t0=t0: e.dma_start(
                        out=out_fm.rearrange("(c p) t -> p c t", p=128)[:, :, t0:t0 + MT],
                        in_=res[:, :, :]))
    else:
        ser.dmag(lambda e: e.dma_start(
            out=out_fm.rearrange("(c p) t -> p c t", p=128),
            in_=xs[:, :, 1:1 + T]))



from concourse.bass_utils import run_bass_kernel_spmd

NCORES = 8


def _host_prep(inputs):
    g = inputs
    h = {}
    h["w_in_t"] = np.ascontiguousarray(g["L0_attn_W_in"].T)
    h["w_out_t"] = np.ascontiguousarray(g["L0_attn_W_out"].T)
    WxT = np.ascontiguousarray(g["L0_attn_W_x"].T)
    w_x_aug = np.zeros((2048, 65), np.float32)
    w_x_aug[:, 0:16] = WxT[:, 1:17]
    w_x_aug[:, 32:48] = WxT[:, 17:33]
    w_x_aug[:, 64] = WxT[:, 0]
    h["w_x_aug"] = w_x_aug
    h["convw_s"] = np.ascontiguousarray(
        g["L0_attn_conv_w"][:, 0, :].reshape(16, 128, 4).transpose(1, 0, 2))
    h["convb_s"] = np.ascontiguousarray(g["L0_attn_conv_b"].reshape(16, 128).T)
    h["d_s"] = np.ascontiguousarray(g["L0_attn_D"].reshape(16, 128).T)
    h["alog_s"] = np.ascontiguousarray(g["L0_attn_A_log"].reshape(1, 16))
    h["ones_1p"] = np.ones((1, 128), np.float32)
    h["ones16"] = np.ones((16, 1), np.float32)
    h["ones128"] = np.ones((128, 1), np.float32)
    h["w1t"] = np.ascontiguousarray(g["L0_ffn_W1"].T)
    h["w2t"] = np.ascontiguousarray(g["L0_ffn_W2"].T)
    h["b1_s"] = np.ascontiguousarray(g["L0_ffn_b1"].reshape(32, 128).T)
    h["b2_s"] = np.ascontiguousarray(g["L0_ffn_b2"].reshape(8, 128).T)
    h["w_write_t"] = np.ascontiguousarray(g["L1_attn_W_write"].T)
    h["w_read_t"] = np.ascontiguousarray(g["L1_attn_W_read"].T)
    waug = np.zeros((1024, 72), np.float32)
    waug[:, 0:8] = g["L1_attn_W_alpha"].T
    waug[:, 64:72] = g["L1_attn_W_beta"].T
    h["waug"] = waug
    h["alog8"] = np.ascontiguousarray(g["L1_attn_A_log"].reshape(8, 1))
    h["dtb8"] = np.ascontiguousarray(g["L1_attn_dt_bias"].reshape(8, 1))
    h["trilneg"] = (-np.tril(np.ones((128, 128)), -1)).astype(np.float32)
    h["triu1"] = np.triu(np.ones((128, 128)), 1).astype(np.float32)
    h["ident"] = np.eye(128, dtype=np.float32)
    h["ones2d"] = np.ones((128, 128), np.float32)
    DEPTH = 6
    W1 = g["L1_ffn_W1"]
    W2 = g["L1_ffn_W2"]
    b1 = g["L1_ffn_b1"]
    w1a = np.zeros((DEPTH, 1024, 2816), np.float32)
    w1g = np.zeros((DEPTH, 1024, 2816), np.float32)
    w1a[:, :, 0:2730] = W1[:, :, 0:2730]
    w1g[:, :, 0:2730] = W1[:, :, 2730:5460]
    w2p = np.zeros((DEPTH, 2816, 1024), np.float32)
    w2p[:, 0:2730] = W2
    b1a = np.zeros((DEPTH, 2816), np.float32)
    b1g = np.zeros((DEPTH, 2816), np.float32)
    b1a[:, 0:2730] = b1[:, 0:2730]
    b1g[:, 0:2730] = b1[:, 2730:5460]
    tr = lambda a, nch: np.ascontiguousarray(
        a.reshape(DEPTH, nch, 128).transpose(2, 0, 1))
    h["wqkv"] = g["L1_ffn_Wqkv"]
    h["wo"] = g["L1_ffn_Wo"]
    h["w1a"], h["w1g"], h["w2p"] = w1a, w1g, w2p
    h["b1a_s"], h["b1g_s"] = tr(b1a, 22), tr(b1g, 22)
    h["b2_s2"] = tr(g["L1_ffn_b2"], 8)
    h["lng_s"] = tr(g["L1_ffn_ln_g"], 8)
    h["lnb_s"] = tr(g["L1_ffn_ln_b"], 8)
    h["qs_s"] = np.ascontiguousarray(
        g["L1_ffn_q_scale"].reshape(DEPTH, 1, 64).transpose(1, 0, 2))
    h["ks_s"] = np.ascontiguousarray(
        g["L1_ffn_k_scale"].reshape(DEPTH, 1, 64).transpose(1, 0, 2))
    DH, Wn = 64, 256
    half = np.arange(0, DH, 2, dtype=np.float64)
    invf = 1.0 / 10000.0 ** (half / DH)
    t = np.arange(T, dtype=np.float64)
    freqs = np.concatenate([t[:, None] * invf[None, :]] * 2, -1)
    bs = (half + 0.4 * DH) / (1.4 * DH)
    power = (t - T // 2) / (Wn // 2)
    sc = np.concatenate([bs[None, :] ** power[:, None]] * 2, -1)
    cos, sin = np.cos(freqs), np.sin(freqs)
    rep = lambda a: np.ascontiguousarray(np.tile(a.T, (2, 1)).astype(np.float32))
    h["csq8"] = rep(cos * sc * 8.0)
    h["ssq8"] = rep(sin * sc * 8.0)
    h["csk"] = rep(cos / sc)
    h["ssk"] = rep(sin / sc)
    prot = np.zeros((128, 128), np.float32)
    for blk in range(2):
        for f in range(32):
            prot[blk * 64 + f, blk * 64 + f + 32] = -1.0
            prot[blk * 64 + f + 32, blk * 64 + f] = 1.0
    h["protT"] = np.ascontiguousarray(prot.T)
    NEG = -60000.0
    mstd = np.full((128, 384), NEG, np.float32)
    for r in range(128):
        mstd[r, r:r + 257] = 0.0
    mq0 = np.full((128, 128), NEG, np.float32)
    for r in range(128):
        mq0[r, 0:r + 1] = 0.0
    mq1 = np.full((128, 256), NEG, np.float32)
    for r in range(128):
        mq1[r, 0:r + 129] = 0.0
    h["mstd"], h["mq0"], h["mq1"] = mstd, mq0, mq1
    h["mstdT"] = np.ascontiguousarray(np.concatenate(
        [mstd[:, 256:384].T, mstd[:, 128:256].T, mstd[:, 0:128].T], axis=1))
    h["mq0T"] = np.ascontiguousarray(mq0.T)
    h["mq1T"] = np.ascontiguousarray(np.concatenate(
        [mq1[:, 0:128].T, mq1[:, 128:256].T], axis=1))
    return h


def _declare(nc, host, names, renames=None):
    renames = renames or {}
    dram = {}
    for k in names:
        v = host[k]
        dram[renames.get(k, k)] = nc.dram_tensor(k, list(v.shape), F32,
                                                 kind="ExternalInput")
    return dram


def _run(nc, host, names, x_list, renames=None):
    in_maps = [dict({k: host[k] for k in names}, x_fm=x_list[i])
               for i in range(NCORES)]
    res = run_bass_kernel_spmd(nc, in_maps, core_ids=list(range(NCORES)))
    return [np.asarray(r["out_fm"], np.float32) for r in res.results]


SEM_BASE = 190
from contextlib import ExitStack


def _emit_prog(nc, ser, launch_idx):
    finals = dict.fromkeys(Ser.STREAMS, 0)
    for (en, pos, inc, fn, after) in ser.ops:
        finals[en] += inc
    with ExitStack() as st:
        semgrid = {}
        num = SEM_BASE
        for y in Ser.STREAMS:
            n = finals[y] // Ser.SEG + 1
            semgrid[y] = [st.enter_context(nc.semaphore(f"c_{y}{j}", num + j))
                          for j in range(n)]
            num += n
        block = st.enter_context(nc.Block())
        ser.emit(block, semgrid)


def kernel(**inputs):
    inputs = {k: np.asarray(v, np.float32) for k, v in inputs.items()}
    host = _host_prep(inputs)
    x0 = inputs["L0_attn_x"]
    x_list = [np.ascontiguousarray(x0[i].T) for i in range(NCORES)]

    names = ["w_in_t", "w_out_t", "w_x_aug", "convw_s", "convb_s", "d_s",
             "alog_s", "ones_1p", "ones16", "ones128",
             "w1t", "w2t", "b1_s", "b2_s",
             "w_write_t", "w_read_t", "waug", "alog8", "dtb8", "trilneg",
             "triu1", "ident", "ones2d",
             "wqkv", "wo", "w1a", "w1g", "w2p", "b1a_s", "b1g_s", "b2_s2",
             "lng_s", "lnb_s", "qs_s", "ks_s", "csq8", "ssq8", "csk", "ssk",
             "protT", "mstd", "mq0", "mq1", "mstdT", "mq0T", "mq1T"]
    nc = bass.Bass("TRN2", target_bir_lowering=False)
    dram = _declare(nc, host, names, renames={"b2_s2": "b2_s_d"})
    x_in = nc.dram_tensor("x_fm", [DM, T], F32, kind="ExternalInput")
    out = nc.dram_tensor("out_fm", [DM, T], F32, kind="ExternalOutput")
    xi1 = nc.dram_tensor("xi1", [DM, T], F32, kind="Internal")
    xi2 = nc.dram_tensor("xi2", [DM, T], F32, kind="Internal")
    xi3 = nc.dram_tensor("xi3", [DM, T], F32, kind="Internal")
    rows_scratch = nc.dram_tensor("rows_scratch", [2, 8, 1024], F32, kind="Internal")
    stat_scratch = nc.dram_tensor("stat_scratch", [128, 128], F32, kind="Internal")

    ser = Ser(nc)
    C = Common(nc)
    C.load(ser, dram)
    mark = nc.sbuf_base

    dA_ = dict(dram, x_fm=x_in, out_fm=xi1)
    build_l0_attn(ser, nc, C, dA_)
    nc.sbuf_base = mark

    dB_ = dict(dram, x_fm=xi1, out_fm=xi2)
    build_l0_ffn(ser, nc, C, dB_)
    nc.sbuf_base = mark

    dC_ = dict(dram, x_fm=xi2, out_fm=xi3, rows_scratch=rows_scratch)
    build_l1_attn(ser, nc, C, dC_)
    nc.sbuf_base = mark

    dD_ = dict(dram, x_fm=xi3, out_fm=out, x_orig=xi3,
               stat_scratch=stat_scratch, b2_s=dram["b2_s_d"])
    build_l1_ffn(ser, nc, C, dD_, layers=[0, 1, 2, 3, 4, 5], final_ln=True)

    _emit_prog(nc, ser, 0)

    in_maps = [dict({k: host[k] for k in names}, x_fm=x_list[i])
               for i in range(NCORES)]
    res = run_bass_kernel_spmd(nc, in_maps, core_ids=list(range(NCORES)))
    x_list = [np.asarray(r["out_fm"], np.float32) for r in res.results]
    outa = np.stack([x_list[i].T for i in range(NCORES)], axis=0)
    return np.ascontiguousarray(outa.astype(np.float32))


def kernel_unfused(**inputs):
    inputs = {k: np.asarray(v, np.float32) for k, v in inputs.items()}
    host = _host_prep(inputs)
    x0 = inputs["L0_attn_x"]
    x_list = [np.ascontiguousarray(x0[i].T) for i in range(NCORES)]

    namesA = ["w_in_t", "w_out_t", "w_x_aug", "convw_s", "convb_s", "d_s",
              "alog_s", "ones_1p", "ones16", "ones128"]
    nc = bass.Bass("TRN2", target_bir_lowering=False)
    dram = _declare(nc, host, namesA)
    dram["x_fm"] = nc.dram_tensor("x_fm", [DM, T], F32, kind="ExternalInput")
    dram["out_fm"] = nc.dram_tensor("out_fm", [DM, T], F32, kind="ExternalOutput")
    ser = Ser(nc)
    build_l0_attn(ser, nc, dram)
    _emit_prog(nc, ser, 0)
    x_list = _run(nc, host, namesA, x_list)

    namesB = ["w1t", "w2t", "b1_s", "b2_s", "ones_1p", "ones128"]
    nc = bass.Bass("TRN2", target_bir_lowering=False)
    dram = _declare(nc, host, namesB)
    dram["x_fm"] = nc.dram_tensor("x_fm", [DM, T], F32, kind="ExternalInput")
    dram["out_fm"] = nc.dram_tensor("out_fm", [DM, T], F32, kind="ExternalOutput")
    ser = Ser(nc)
    C = Common(nc)
    C.load(ser, dram)
    build_l0_ffn(ser, nc, C, dram)
    _emit_prog(nc, ser, 1)
    x_list = _run(nc, host, namesB, x_list)

    namesC = ["w_write_t", "w_read_t", "waug", "alog8", "dtb8", "trilneg",
              "triu1", "ident", "ones2d", "ones_1p", "ones128"]
    nc = bass.Bass("TRN2", target_bir_lowering=False)
    dram = _declare(nc, host, namesC)
    dram["x_fm"] = nc.dram_tensor("x_fm", [DM, T], F32, kind="ExternalInput")
    dram["out_fm"] = nc.dram_tensor("out_fm", [DM, T], F32, kind="ExternalOutput")
    dram["rows_scratch"] = nc.dram_tensor("rows_scratch", [2, 8, 1024], F32,
                                          kind="Internal")
    ser = Ser(nc)
    C = Common(nc)
    C.load(ser, dram)
    build_l1_attn(ser, nc, C, dram)
    _emit_prog(nc, ser, 2)
    x_list = _run(nc, host, namesC, x_list)

    namesD = ["wqkv", "wo", "w1a", "w1g", "w2p", "b1a_s", "b1g_s", "b2_s2",
              "lng_s", "lnb_s", "qs_s", "ks_s", "csq8", "ssq8", "csk", "ssk",
              "protT", "mstd", "mq0", "mq1", "mstdT", "mq0T", "mq1T",
              "ident", "ones_1p", "ones128"]
    groups = [[0, 1, 2], [3, 4, 5]]
    x3_list = x_list
    for gi, layers in enumerate(groups):
        last = gi == len(groups) - 1
        nc = bass.Bass("TRN2", target_bir_lowering=False)
        dram = _declare(nc, host, namesD, renames={"b2_s2": "b2_s"})
        dram["x_fm"] = nc.dram_tensor("x_fm", [DM, T], F32, kind="ExternalInput")
        dram["out_fm"] = nc.dram_tensor("out_fm", [DM, T], F32,
                                        kind="ExternalOutput")
        dram["stat_scratch"] = nc.dram_tensor("stat_scratch", [128, 128], F32,
                                              kind="Internal")
        if last:
            dram["x_orig"] = nc.dram_tensor("x_orig", [DM, T], F32,
                                            kind="ExternalInput")
        ser = Ser(nc)
        C = Common(nc)
        C.load(ser, dram)
        build_l1_ffn(ser, nc, C, dram, layers=layers, final_ln=last)
        _emit_prog(nc, ser, 3 + gi)
        if last:
            in_maps = [dict({k: host[k] for k in namesD}, x_fm=x_list[i],
                            x_orig=x3_list[i]) for i in range(NCORES)]
            res_ = run_bass_kernel_spmd(nc, in_maps, core_ids=list(range(NCORES)))
            x_list = [np.asarray(r["out_fm"], np.float32) for r in res_.results]
        else:
            x_list = _run(nc, host, namesD, x_list)

    out = np.stack([x_list[i].T for i in range(NCORES)], axis=0)
    return np.ascontiguousarray(out.astype(np.float32))
